# version 11 (final)
# speedup vs baseline: 1.0253x; 1.0036x over previous
import math
import contextlib
import numpy as np
import ml_dtypes
import concourse.bass as bass
import concourse.mybir as mybir
from concourse.bass_utils import run_bass_kernel_spmd

F32 = mybir.dt.float32
BF16 = mybir.dt.bfloat16
I32 = mybir.dt.int32
AF = mybir.ActivationFunctionType
ALU = mybir.AluOpType
AX = mybir.AxisListType

NCORES = 8
B = 2
T = 2048
C = 1024
H = 16
D = 64
TTN = T // 128
CCN = C // 128
I0 = 3072
I1 = 2816
SHIFT0 = 16.0
LN_EPS = 1e-5

MAGIC = 12582912.0
CPV = [0.0, 1.0, SHIFT0, -SHIFT0, 1e-12, 1e-8, LN_EPS, 0.25, -1.0,
       MAGIC, -MAGIC]
(CP_ZERO, CP_ONE, CP_SHIFT0, CP_NSHIFT0, CP_EPS12, CP_EPS8, CP_LNEPS,
 CP_QTR, CP_NONE, CP_MAG, CP_NMAG) = range(len(CPV))

SLOPES = [2.0 ** (-(8.0 / H) * (h + 1)) for h in range(H)]
NC_CC = [NCORES]

AR2_COLS = 81280
AR4_COLS = 8512


class Walk:

    def __init__(self, which, eng):
        self.which = which
        self.eng = eng
        self.c = {}

    def on(self, engname):
        return self.which == engname

    def bump(self, semh, n=1):
        self.c[id(semh)] = self.c.get(id(semh), 0) + n

    def val(self, semh):
        return self.c.get(id(semh), 0)

    def wait(self, engname, semh, v=None):
        if v is None:
            v = self.val(semh)
        if v > 0 and self.which == engname:
            self.eng.wait_ge(semh, v)


class PingPong:

    def __init__(self, walk, bufs, ld_sems, free_sem, producer, consumer,
                 ld_step=16):
        self.w = walk
        self.ld_base = [walk.val(s) for s in ld_sems]
        self.free_base = walk.val(free_sem)
        self.bufs = bufs
        self.ld = ld_sems
        assert len(ld_sems) == len(bufs)
        self.free = free_sem
        self.producer = producer
        self.consumer = consumer
        self.ld_step = ld_step
        self.n_load = 0
        self.n_cons = 0

    def slot(self):
        return self.bufs[self.n_load % len(self.bufs)]

    def pre_load(self):
        if self.n_load >= len(self.bufs):
            self.w.wait(self.producer, self.free,
                        self.free_base + self.n_load - len(self.bufs) + 1)

    def ld_sem(self):
        return self.ld[self.n_load % len(self.bufs)]

    def did_load(self):
        self.w.bump(self.ld[self.n_load % len(self.bufs)], self.ld_step)
        self.n_load += 1

    def cslot(self):
        return self.bufs[self.n_cons % len(self.bufs)]

    def pre_consume(self):
        k = self.n_cons % len(self.bufs)
        uses = self.n_cons // len(self.bufs) + 1
        self.w.wait(self.consumer, self.ld[k],
                    self.ld_base[k] + uses * self.ld_step)

    def did_consume(self, ins=None):
        self.n_cons += 1
        if self.w.on(self.consumer):
            (ins if ins is not None else self.w.eng.nop()).then_inc(self.free, 1)
        self.w.bump(self.free, 1)


def build_program(n_macro=99, taps=()):
    nc = bass.Bass("TRN2", target_bir_lowering=False)
    taps = set(taps)

    def inp(name, shape, dtype=F32):
        return nc.dram_tensor(name, shape, dtype, kind="ExternalInput").ap()

    ext = {}
    for name, shape, dt in [
        ("x", [B, T, C], F32), ("w_attn", [C, 3 * C], F32),
        ("bqk_q", [128, H // 2], F32), ("bqk_k", [128, H // 2], F32),
        ("bv_pairs", [1, 8 * 128], F32),
        ("w_proj", [C, C], F32), ("bproj_cols", [128, CCN], F32),
        ("f0_gate", [I0, C], F32), ("f0_up", [I0, C], F32),
        ("f0_down", [C, I0], F32), ("c0", [B, C], F32),
        ("wmod0", [C, 6 * C], F32), ("wk1", [C, C], F32), ("wv1", [C, C], F32),
        ("wq1", [C, C], F32), ("wbeta1", [C, H], F32), ("wout1", [C, C], F32),
        ("f1_gate", [I1, C], F32), ("f1_up", [I1, C], F32),
        ("f1_down", [C, I1], F32), ("c1", [B, C], F32), ("wmod1", [C, 6 * C], F32),
        ("consts", [1, len(CPV)], F32), ("onesrow", [1, 128], F32),
        ("identb", [128, 128], BF16), ("identf", [128, 128], F32),
        ("iota_col", [128, 1], F32), ("iota_delta", [128, 16], F32),
        ("sel2", [128, 2], BF16), ("sel2b", [2, 128], F32),
        ("selcli", [17, 32], F32), ("selcli2", [17, 32], F32),
        ("mask_ut", [128, 128], BF16), ("mask_lt", [128, 128], BF16),
    ]:
        ext[name] = inp(name, shape, dt)

    out = nc.dram_tensor("out", [B, T, C], F32, kind="ExternalOutput").ap()

    sc = {}
    sc["X"] = nc.dram_tensor("X", [B, T, C], F32, kind="Internal").ap()
    sc["H2T"] = nc.dram_tensor("H2T", [B, CCN, 128, T], BF16, kind="Internal").ap()
    sc["UP"] = nc.dram_tensor("UPs", [B, I0 // 128, 128, T], BF16, kind="Internal").ap()
    sc["GT"] = nc.dram_tensor("GTs", [B, I0 // 128, 128, T], BF16, kind="Internal").ap()
    sc["G2"] = nc.dram_tensor("G2s", [B, I0 // 128, 128, T], BF16, kind="Internal").ap()
    sc["TO"] = nc.dram_tensor("TOs", [B, I0 // 128, 128, T], BF16, kind="Internal").ap()
    _ccn = [1, 2, 1, 1]
    sc["cc_in"] = [nc.dram_tensor(f"cc_in{k}", [1, _ccn[k]], F32,
                                  kind="Internal").ap() for k in range(4)]
    sc["cc_out"] = [nc.dram_tensor(f"cc_out{k}", [1, _ccn[k]], F32,
                                   kind="Internal").ap() for k in range(4)]
    tap_aps = {}
    for tn in taps:
        tap_aps[tn] = nc.dram_tensor("tap_" + tn, [B, T, C], F32,
                                     kind="ExternalOutput").ap()

    with contextlib.ExitStack() as es:
        _emit(nc, es, ext, sc, out, tap_aps, n_macro)
    return nc


def _emit(nc, es, ext, sc, out, tap_aps, n_macro):
    def sbuf(name, shape, dtype=F32):
        return es.enter_context(nc.sbuf_tensor(name, shape, dtype))

    def psumt(name, shape, dtype=F32):
        return es.enter_context(nc.psum_tensor(name, shape, dtype))

    def sem(name):
        return es.enter_context(nc.semaphore(name))

    CP = sbuf("CP", [128, len(CPV)])
    modc = [sbuf(f"modc{l}", [128, 6 * CCN * B]) for l in range(2)]

    def mc(l, j, ck, b):
        col = (j * CCN + ck) * B + b
        return modc[l][:, col:col + 1]

    BQKQ = sbuf("BQKQ", [128, H // 2])
    BQKK = sbuf("BQKK", [128, H // 2])
    BPROJ = sbuf("BPROJ", [128, CCN])
    IDENTB = sbuf("IDENTB", [128, 128], BF16)
    IDENTF = sbuf("IDENTF", [128, 128])
    ONESROW = sbuf("ONESROW", [1, 128])
    IOTACOL = sbuf("IOTACOL", [128, 1])
    IOTADEL = sbuf("IOTADEL", [128, 16])
    MASKUT = sbuf("MASKUT", [128, 128], BF16)
    MASKLT = sbuf("MASKLT", [128, 128], BF16)
    COL16 = sbuf("COL16", [128, 16 * H])
    DESC = sbuf("DESC", [128, H])
    SEL2 = sbuf("SEL2", [128, 2], BF16)
    SEL2B = sbuf("SEL2B", [2, 128])
    SELCLI = sbuf("SELCLI", [49, 32])
    SELCLI2 = sbuf("SELCLI2", [49, 32])

    BARR = sbuf("BARR", [1, 8])
    AR2 = sbuf("AR2", [128, AR2_COLS], BF16)
    AR4 = sbuf("AR4", [128, AR4_COLS])
    AR4X = sbuf("AR4X", [128, 648])
    BVP = AR4[0:1, 6208:7232]
    ARI = None
    PSF = [psumt(f"psf{i}", [128, 512]) for i in range(6)]
    PSB = [psumt(f"psb{i}", [128, 1024], BF16) for i in range(2)]

    phases = []
    PH = sem("PH")
    _semreg = {}

    def getsem(name):
        if name not in _semreg:
            _semreg[name] = sem(name)
        return _semreg[name]

    def add_phase(name, walkfn):
        phases.append((name, walkfn))

    shared = dict(nc=nc, sem=getsem, ext=ext, sc=sc, out=out, tap_aps=tap_aps,
                  CP=CP, mc=mc, modc=modc, BQKQ=BQKQ, BQKK=BQKK, BVP=BVP,
                  AR4X=AR4X,
                  BPROJ=BPROJ,
                  IDENTB=IDENTB, IDENTF=IDENTF, ONESROW=ONESROW,
                  IOTACOL=IOTACOL, IOTADEL=IOTADEL, MASKUT=MASKUT,
                  SEL2=SEL2, SEL2B=SEL2B, SELCLI=SELCLI, SELCLI2=SELCLI2,
                  MASKLT=MASKLT, COL16=COL16, DESC=DESC, AR2=AR2, AR4=AR4,
                  ARI=ARI, PSF=PSF, PSB=PSB, add_phase=add_phase)

    _p0(shared)
    if n_macro >= 1:
        _l0_attn(shared)
    if n_macro >= 2:
        _l0_ffn(shared)
    if n_macro >= 3:
        _l1_attn(shared)
    if n_macro >= 4:
        _l1_ffn(shared)

    with nc.Block() as block:
        def run_all(which):
            def _go(eng):
                wk = Walk(which, eng)
                tgt = 0
                for name, walkfn in phases:
                    walkfn(wk)
                    if which == "sync":
                        eng.nop().then_inc(PH, 1)
                    elif which == "tensor":
                        eng.drain()
                        eng.wait_ge(PH, tgt + 4)
                        eng.matmul(PSF[5][0:1, 0:1], ONESROW[0:1, 0:1],
                                   ONESROW[0:1, 0:1], start=True, stop=True
                                   ).then_inc(PH, 1)
                    elif which == "vector":
                        eng.memset(BARR[0:1, 0:1], 0.0).then_inc(PH, 1)
                    elif which == "scalar":
                        eng.activation(out=BARR[0:1, 1:2], in_=CP[0:1, 0:1],
                                       func=AF.Identity, scale=0.0
                                       ).then_inc(PH, 1)
                    else:
                        eng.memset(BARR[0:1, 2:3], 0.0).then_inc(PH, 1)
                    tgt += 5
                    eng.wait_ge(PH, tgt)
            return _go

        block.sync(run_all("sync"))
        block.tensor(run_all("tensor"))
        block.vector(run_all("vector"))
        block.scalar(run_all("scalar"))
        block.gpsimd(run_all("gpsimd"))



def _p0(S):
    nc, sem, ext = S["nc"], S["sem"], S["ext"]
    CP, modc, IDENTF, ONESROW = S["CP"], S["modc"], S["IDENTF"], S["ONESROW"]
    IOTACOL, IOTADEL, COL16, DESC = (S["IOTACOL"], S["IOTADEL"], S["COL16"],
                                     S["DESC"])
    AR4, PSF = S["AR4"], S["PSF"]
    add_phase = S["add_phase"]

    def a4(cols, o=[0]):
        s = o[0]
        o[0] += cols
        assert o[0] <= AR4_COLS
        return S["AR4"][:, s:s + cols]

    p0_crow = a4(len(CPV))[0:1, :]
    p0_c = a4(C)[0:B, :]
    p0_cs = a4(C)[0:B, :]
    p0_csT = a4(CCN * B)
    p0_wmod = [a4(3072) for _ in range(2)]
    p0_t0 = a4(CCN * B)
    p0_t1 = a4(CCN * B)

    s0 = {k: sem(k) for k in ("ld", "wm", "wm2", "wmc", "v", "s",
                              "t", "wmf")}

    def walk(w):
        eng = w.eng
        ld, wm, sv, ss, st, wmf = (s0["ld"], s0["wm"], s0["v"], s0["s"],
                                   s0["t"], s0["wmf"])
        wm2, wmc = s0["wm2"], s0["wmc"]
        small = [(p0_crow, ext["consts"]), (ONESROW[:], ext["onesrow"]),
                 (S["IDENTB"][:], ext["identb"]), (IDENTF[:], ext["identf"]),
                 (IOTACOL[:], ext["iota_col"]), (IOTADEL[:], ext["iota_delta"]),
                 (S["MASKUT"][:], ext["mask_ut"]), (S["MASKLT"][:], ext["mask_lt"]),
                 (S["BQKQ"][:], ext["bqk_q"]), (S["BQKK"][:], ext["bqk_k"]),
                 (S["SEL2"][:], ext["sel2"]), (S["SEL2B"][:], ext["sel2b"]),
                 (S["SELCLI"][32:49, :], ext["selcli"]),
                 (S["SELCLI2"][32:49, :], ext["selcli2"]),
                 (S["BPROJ"][:], ext["bproj_cols"])]
        for o, i in small:
            if w.on("sync"):
                eng.dma_start(out=o, in_=i).then_inc(ld, 16)
            w.bump(ld, 16)
        w.wait("tensor", ld)
        if w.on("tensor"):
            eng.matmul(PSF[0][:, 0:len(CPV)], ONESROW[:], p0_crow,
                       start=True, stop=True).then_inc(st, 1)
        w.bump(st)
        w.wait("vector", st)
        if w.on("vector"):
            eng.tensor_copy(out=CP[:], in_=PSF[0][:, 0:len(CPV)]).then_inc(sv, 1)
        w.bump(sv)
        w.wait("scalar", sv)
        w.wait("scalar", ld)
        if w.on("scalar"):
            for h in range(H):
                eng.activation(out=COL16[:, h * 16:(h + 1) * 16], in_=IOTADEL[:],
                               func=AF.Exp, scale=SLOPES[h],
                               bias=CP[:, CP_NSHIFT0:CP_NSHIFT0 + 1])
                ins = eng.activation(out=DESC[:, h:h + 1], in_=IOTACOL[:],
                                     func=AF.Exp, scale=-SLOPES[h],
                                     bias=CP[:, CP_SHIFT0:CP_SHIFT0 + 1])
            ins.then_inc(ss, 1)
        w.bump(ss)
        nload = [0]
        for l in range(2):
            cin = ext["c0"] if l == 0 else ext["c1"]
            wmext = ext["wmod0"] if l == 0 else ext["wmod1"]
            w.wait("sync", st)
            if w.on("sync"):
                eng.dma_start(out=p0_c, in_=cin).then_inc(wmc, 16)
            w.bump(wmc, 16)
            w.wait("scalar", wmc)
            if w.on("scalar"):
                eng.activation(out=p0_cs, in_=p0_c, func=AF.Silu).then_inc(ss, 1)
            w.bump(ss)
            w.wait("tensor", ss)
            if w.on("tensor"):
                for ck in range(CCN):
                    ins = eng.matmul(PSF[2][:, ck * B:(ck + 1) * B],
                                     p0_cs[0:B, ck * 128:(ck + 1) * 128],
                                     IDENTF[0:B, 0:B], is_transpose=True,
                                     start=(ck == 0), stop=(ck == CCN - 1))
                ins.then_inc(st, 1)
            w.bump(st)
            w.wait("vector", st)
            if w.on("vector"):
                eng.tensor_copy(out=p0_csT, in_=PSF[2][:, 0:CCN * B]).then_inc(sv, 1)
            w.bump(sv)
            nmm = 0
            for ck in range(CCN):
                for half in range(2):
                    if nload[0] >= 2:
                        w.wait("sync", wmf, nload[0] - 1)
                    wsem = wm if nload[0] % 2 == 0 else wm2
                    if w.on("sync"):
                        eng.dma_start(out=p0_wmod[nload[0] % 2][:],
                                      in_=wmext[ck * 128:(ck + 1) * 128,
                                                half * 3072:(half + 1) * 3072]
                                      ).then_inc(wsem, 16)
                    slot = nload[0] % 2
                    nload[0] += 1
                    w.bump(wsem, 16)
                    w.wait("tensor", wsem)
                    w.wait("tensor", sv)
                    if w.on("tensor"):
                        for mk in range(24):
                            mchunk = half * 24 + mk
                            ins = eng.matmul(
                                PSF[5][:, mchunk * B:(mchunk + 1) * B],
                                p0_wmod[slot][:, mk * 128:(mk + 1) * 128],
                                p0_csT[:, ck * B:(ck + 1) * B],
                                start=(nmm == 0),
                                stop=(nmm == CCN * 2 * 24 - 1))
                            nmm += 1
                        ins.then_inc(wmf, 1)
                    else:
                        nmm += 24
                    w.bump(wmf, 1)
            w.wait("vector", wmf)
            if w.on("vector"):
                eng.tensor_copy(out=modc[l][:], in_=PSF[5][:, 0:6 * CCN * B])
                eng.drain()
                for j in (1, 4):
                    g = modc[l][:, j * CCN * B:(j + 1) * CCN * B]
                    eng.tensor_scalar(out=g, in0=g, scalar1=1.0, scalar2=None,
                                      op0=ALU.add)
                for j in (2, 5):
                    g = modc[l][:, j * CCN * B:(j + 1) * CCN * B]
                    t0 = p0_t0
                    t1 = p0_t1
                    eng.drain()
                    eng.tensor_scalar(out=t0, in0=g, scalar1=0.0, scalar2=2.0,
                                      op0=ALU.is_ge, op1=ALU.mult)
                    eng.scalar_tensor_tensor(out=t1, in0=g, scalar=-1.0, in1=g,
                                             op0=ALU.mult, op1=ALU.max)
                    eng.drain()
                    eng.tensor_scalar(out=t0, in0=t0, scalar1=-1.0,
                                      scalar2=None, op0=ALU.add)
                    eng.tensor_scalar(out=t1, in0=t1, scalar1=0.25,
                                      scalar2=None, op0=ALU.max)
                    eng.drain()
                    ins = eng.tensor_tensor(out=g, in0=t0, in1=t1, op=ALU.mult)
                ins.then_inc(sv, 1)
            w.bump(sv)
            w.wait("tensor", sv)
        return w

    add_phase("P0", walk)



def ln_to_hT(w, eng, sems, xt, h0slot, psb_bank, hT, tt, lncols, junk4,
             IDENTB, sc_col, bi_col):
    if w.on("vector"):
        eng.tensor_reduce(out=lncols[:, 0:1], in_=xt, axis=AX.X, op=ALU.add)
        eng.scalar_tensor_tensor(out=junk4, in0=xt, scalar=1.0, in1=xt,
                                 op0=ALU.mult, op1=ALU.mult,
                                 accum_out=lncols[:, 1:2])
        eng.drain()
        eng.tensor_scalar(out=lncols[:, 2:3], in0=lncols[:, 0:1],
                          scalar1=1.0 / C, scalar2=None, op0=ALU.mult)
        eng.tensor_scalar(out=lncols[:, 4:5], in0=lncols[:, 1:2],
                          scalar1=1.0 / C, scalar2=LN_EPS,
                          op0=ALU.mult, op1=ALU.add)
        eng.drain()
        eng.scalar_tensor_tensor(out=lncols[:, 3:4], in0=lncols[:, 2:3],
                                 scalar=-1.0, in1=lncols[:, 2:3],
                                 op0=ALU.mult, op1=ALU.mult)
        eng.drain()
        eng.tensor_tensor(out=lncols[:, 4:5], in0=lncols[:, 4:5],
                          in1=lncols[:, 3:4], op=ALU.add).then_inc(sems["st"], 1)
    w.bump(sems["st"], 1)
    w.wait("scalar", sems["st"])
    if w.on("scalar"):
        eng.activation(out=lncols[:, 5:6], in_=lncols[:, 4:5], func=AF.Ln)
        eng.drain()
        eng.activation(out=lncols[:, 6:7], in_=lncols[:, 5:6], func=AF.Exp,
                       scale=-0.5).then_inc(sems["inv"], 1)
    w.bump(sems["inv"], 1)
    w.wait("vector", sems["inv"])
    if w.on("vector"):
        eng.scalar_tensor_tensor(out=lncols[:, 7:8], in0=lncols[:, 2:3],
                                 scalar=-1.0, in1=lncols[:, 6:7],
                                 op0=ALU.mult, op1=ALU.mult)
        eng.drain()
        eng.tensor_scalar(out=h0slot, in0=xt, scalar1=lncols[:, 6:7],
                          scalar2=lncols[:, 7:8], op0=ALU.mult, op1=ALU.add).then_inc(sems["h0"], 1)
    w.bump(sems["h0"], 1)
    w.wait("tensor", sems["h0"])
    if w.on("tensor"):
        for ck in range(CCN):
            ins = eng.matmul(psb_bank[:, ck * 128:(ck + 1) * 128],
                             h0slot[:, ck * 128:(ck + 1) * 128], IDENTB[:],
                             is_transpose=True,
                             start=(ck == 0), stop=(ck == CCN - 1))
        ins.then_inc(sems["trA"], 1)
    w.bump(sems["trA"], 1)
    w.wait("scalar", sems["trA"])
    if w.on("scalar"):
        for ck in range(CCN):
            ins = eng.activation(
                out=hT[:, ck * T + tt * 128:ck * T + (tt + 1) * 128],
                in_=psb_bank[:, ck * 128:(ck + 1) * 128],
                func=AF.Identity, scale=sc_col(ck), bias=bi_col(ck))
        ins.then_inc(sems["hTr"], 1)
    w.bump(sems["hTr"], 1)



def _l0_attn(S):
    nc, sem, ext, sc, tap_aps = S["nc"], S["sem"], S["ext"], S["sc"], S["tap_aps"]
    CP, mc = S["CP"], S["mc"]
    BQKQ, BQKK, BVP, BPROJ = S["BQKQ"], S["BQKK"], S["BVP"], S["BPROJ"]
    IDENTB, ONESROW = S["IDENTB"], S["ONESROW"]
    COL16, DESC, MASKUT = S["COL16"], S["DESC"], S["MASKUT"]
    AR2, AR4, PSF, PSB = S["AR2"], S["AR4"], S["PSF"], S["PSB"]
    add_phase = S["add_phase"]

    o2 = [0]

    def a2(cols):
        s = o2[0]
        o2[0] += cols
        assert o2[0] <= AR2_COLS, f"AR2 overflow {o2[0]}"
        return AR2[:, s:s + cols]

    o4 = [0]

    def a4(cols):
        s = o4[0]
        o4[0] += cols
        assert o4[0] <= AR4_COLS, f"AR4 overflow {o4[0]}"
        return AR4[:, s:s + cols]

    wA = a2(CCN * 3 * C)
    hT = a2(CCN * T)
    AT = a2(CCN * T)
    POFF = [0]
    for j in range(TTN):
        POFF.append(POFF[-1] + T - 128 * j)
    Pbuf = a2(POFF[-1])
    wP = Pbuf[:, 0:CCN * C]
    QTb = a2(T)
    KTb = a2(T)
    vtm2 = [a2(TTN * 65), a2(TTN * 65)]
    vtl = [a2(65), a2(65)]
    h0b = [vtm2[0][:, 0:C], vtm2[1][:, 0:C]]
    yfb = Pbuf[:, CCN * C:CCN * C + CCN * 512]
    aot = a2(64)
    bvs2 = a2(128)

    xa = [a4(C), a4(C)]
    wstg = [a4(1024), a4(1024)]
    odb = a4(TTN * 65)
    rin16 = a4(TTN)
    lncols = a4(8)
    junk4 = a4(C)

    s1 = {k: sem(k) for k in
          ("xld", "xld2", "wld", "wld2", "xfree", "wfree",
           "lnst", "lninv", "h0", "trA",
           "hTr", "qk", "qke", "bv", "vt", "vte", "sgrp", "sexp", "smsk",
           "pvv", "pvm", "od", "rin", "ao", "aoT", "atc", "pj", "pje",
           "tr2", "xnew", "xst", "xst2")}

    ntap = 2 if "X1" in tap_aps else 1

    def walk(w):
        eng = w.eng
        wpp = PingPong(w, wstg, [s1["wld"], s1["wld2"]], s1["wfree"],
                       "sync", "vector")
        xls = [s1["xld"], s1["xld2"]]
        xcnt = [0]

        if w.on("sync"):
            eng.dma_start(out=BVP[:], in_=ext["bv_pairs"]
                          ).then_inc(s1["xfree"], 16)
        w.bump(s1["xfree"], 16)
        w.wait("tensor", s1["xfree"])
        for ck in range(CCN):
            for pc in range(3):
                wpp.pre_load()
                if w.on("sync"):
                    eng.dma_start(out=wpp.slot(),
                                  in_=ext["w_attn"][ck * 128:(ck + 1) * 128,
                                                    pc * 1024:(pc + 1) * 1024]
                                  ).then_inc(wpp.ld_sem(), 16)
                wpp.did_load()
                wpp.pre_consume()
                ins = None
                if w.on("vector"):
                    ins = eng.tensor_copy(
                        out=wA[:, ck * 3 * C + pc * 1024:
                               ck * 3 * C + (pc + 1) * 1024],
                        in_=wpp.cslot())
                wpp.did_consume(ins)


        w.wait("tensor", s1["wfree"])
        for b in range(B):
            _walk_b(w, eng, b, xls, xcnt, wpp)
        return w

    def _walk_b(w, eng, b, xls, xcnt, wpp):
        lsem = {"st": s1["lnst"], "inv": s1["lninv"], "h0": s1["h0"],
                "trA": s1["trA"], "hTr": s1["hTr"]}
        for tt in range(TTN):
            g = b * TTN + tt
            slot = xcnt[0] % 2
            if g >= 2:
                w.wait("sync", s1["h0"], w.val(s1["h0"]) - 1)
            if b > 0 and tt < 2:
                w.wait("sync", s1["xnew"], TTN * b - 1 + tt)
                for xsg in (s1["xst"], s1["xst2"]):
                    w.wait("sync", xsg, w.val(xsg))
            if w.on("sync"):
                eng.dma_start(out=xa[slot][:],
                              in_=ext["x"][b, tt * 128:(tt + 1) * 128, :]
                              ).then_inc(xls[slot], 16)
            w.bump(xls[slot], 16)
            xcnt[0] += 1
            w.wait("vector", xls[slot], (xcnt[0] + 1) // 2 * 16)
            xt = xa[slot][:]
            if g >= 2:
                w.wait("vector", s1["trA"], w.val(s1["trA"]) - 1)
            if g >= 2:
                w.wait("tensor", s1["hTr"], w.val(s1["hTr"]) - 1)
            ln_to_hT(w, eng, lsem, xt, h0b[g % 2][:], PSB[g % 2], hT, tt,
                     lncols, junk4, IDENTB,
                     sc_col=lambda ck: mc(0, 1, ck, b),
                     bi_col=lambda ck: mc(0, 0, ck, b))

        w.wait("vector", s1["trA"])
        if w.on("vector"):
            for sub in range(2):
                for i in range(TTN):
                    ins = eng.memset(vtm2[sub][:, i * 65 + 64:i * 65 + 65], 1.0)
            ins.then_inc(s1["vte"], 1)
        w.bump(s1["vte"], 1)
        for hp in range(H // 2):
            h0_, h1_ = 2 * hp, 2 * hp + 1
            w.wait("tensor", s1["hTr"])
            w.wait("tensor", s1["qke"])
            w.wait("tensor", s1["sexp"])
            if w.on("tensor"):
                for tch in range(4):
                    for ck in range(CCN):
                        ins = eng.matmul(PSF[tch][:, 0:512],
                                         wA[:, ck * 3 * C + h0_ * 64:
                                            ck * 3 * C + h0_ * 64 + 128],
                                         hT[:, ck * T + tch * 512:
                                            ck * T + (tch + 1) * 512],
                                         start=(ck == 0), stop=(ck == CCN - 1))
                ins.then_inc(s1["qk"], 1)
            w.bump(s1["qk"], 1)
            w.wait("vector", s1["qk"])
            if w.on("vector"):
                for tch in range(4):
                    cs = slice(tch * 512, (tch + 1) * 512)
                    for sub in range(2):
                        pp = slice(sub * 64, sub * 64 + 64)
                        ins = eng.tensor_scalar(out=QTb[pp, cs],
                                                in0=PSF[tch][pp, 0:512],
                                                scalar1=BQKQ[pp, hp:hp + 1],
                                                scalar2=0.125,
                                                op0=ALU.add, op1=ALU.mult)
                ins.then_inc(s1["qke"], 1)
            w.bump(s1["qke"], 1)
            w.wait("tensor", s1["qke"])
            if w.on("tensor"):
                for tch in range(4):
                    for ck in range(CCN):
                        ins = eng.matmul(PSF[tch][:, 0:512],
                                         wA[:, ck * 3 * C + C + h0_ * 64:
                                            ck * 3 * C + C + h0_ * 64 + 128],
                                         hT[:, ck * T + tch * 512:
                                            ck * T + (tch + 1) * 512],
                                         start=(ck == 0), stop=(ck == CCN - 1))
                ins.then_inc(s1["qk"], 1)
            w.bump(s1["qk"], 1)
            w.wait("vector", s1["qk"])
            if w.on("vector"):
                for tch in range(4):
                    cs = slice(tch * 512, (tch + 1) * 512)
                    for sub in range(2):
                        pp = slice(sub * 64, sub * 64 + 64)
                        ins = eng.tensor_scalar(out=KTb[pp, cs],
                                                in0=PSF[tch][pp, 0:512],
                                                scalar1=BQKK[pp, hp:hp + 1],
                                                scalar2=None, op0=ALU.add)
                ins.then_inc(s1["qke"], 1)
            w.bump(s1["qke"], 1)
            w.wait("tensor", s1["qke"])
            if w.on("tensor"):
                eng.matmul(PSF[0][:, 0:128], ONESROW[:],
                           BVP[0:1, hp * 128:(hp + 1) * 128],
                           start=True, stop=True).then_inc(s1["bv"], 1)
            w.bump(s1["bv"], 1)
            w.wait("vector", s1["bv"])
            if w.on("vector"):
                ins = eng.tensor_copy(out=bvs2[:], in_=PSF[0][:, 0:128])
                eng.drain()
                ins.then_inc(s1["bv"], 1)
            w.bump(s1["bv"], 1)
            for i in range(TTN):
                bank = PSF[4 + i % 2]
                if i >= 2:
                    w.wait("tensor", s1["vte"], w.val(s1["vte"]) - 2)
                if w.on("tensor"):
                    for ck in range(CCN):
                        ins = eng.matmul(bank[:, 0:128],
                                         hT[:, ck * T + i * 128:
                                            ck * T + (i + 1) * 128],
                                         wA[:, ck * 3 * C + 2 * C + h0_ * 64:
                                            ck * 3 * C + 2 * C + h0_ * 64 + 128],
                                         start=(ck == 0), stop=(ck == CCN - 1))
                    ins.then_inc(s1["vt"], 1)
                w.bump(s1["vt"], 1)
                w.wait("vector", s1["vt"])
                w.wait("vector", s1["bv"])
                if w.on("vector"):
                    for sub in range(2):
                        ins = eng.tensor_tensor(
                            out=vtm2[sub][:, i * 65:i * 65 + 64],
                            in0=bank[:, sub * 64:sub * 64 + 64],
                            in1=bvs2[:, sub * 64:sub * 64 + 64], op=ALU.add)
                    ins.then_inc(s1["vte"], 2)
                w.bump(s1["vte"], 2)
            w.wait("tensor", s1["vte"])
            for h in (h0_, h1_):
                sub = h % 2
                base = sub * 64
                vtm = vtm2[sub]
                for j in range(TTN):
                    for g0 in range(j, TTN, 4):
                        gw = min(4, TTN - g0)
                        gidx = w.val(s1["sgrp"])
                        if gidx >= 2:
                            w.wait("tensor", s1["sexp"], gidx - 1)
                        if w.on("tensor"):
                            for q in range(gw):
                                i = g0 + q
                                ins = eng.matmul(
                                    PSF[gidx % 2][:, q * 128:(q + 1) * 128],
                                    KTb[base:base + 64, j * 128:(j + 1) * 128],
                                    QTb[base:base + 64, i * 128:(i + 1) * 128],
                                    start=(q == 0), stop=(q == gw - 1))
                            ins.then_inc(s1["sgrp"], 1)
                        w.bump(s1["sgrp"], 1)
                        w.wait("scalar", s1["sgrp"])
                        if w.on("scalar"):
                            eng.activation(
                                out=Pbuf[:, POFF[j] + (g0 - j) * 128:
                                         POFF[j] + (g0 - j) * 128 + gw * 128],
                                in_=PSF[gidx % 2][:, 0:gw * 128], func=AF.Exp).then_inc(s1["sexp"], 1)
                        w.bump(s1["sexp"], 1)
                w.wait("vector", s1["sexp"])
                if w.on("vector"):
                    for j in range(TTN):
                        ins = eng.tensor_tensor(
                            out=Pbuf[:, POFF[j]:POFF[j] + 128],
                            in0=Pbuf[:, POFF[j]:POFF[j] + 128],
                            in1=MASKUT[:], op=ALU.mult)
                    ins.then_inc(s1["smsk"], 1)
                w.bump(s1["smsk"], 1)
                w.wait("tensor", s1["smsk"])
                if w.on("vector"):
                    eng.drain()
                for i in range(TTN):
                    bank = PSF[2 + i % 2]
                    if i >= 2:
                        w.wait("tensor", s1["od"], w.val(s1["od"]) - 1)
                    for j in range(i + 1):
                        pvi = w.val(s1["pvv"])
                        if pvi >= 2:
                            w.wait("vector", s1["pvm"], pvi - 1)
                        if w.on("vector"):
                            eng.tensor_scalar(out=vtl[pvi % 2][:],
                                              in0=vtm[:, j * 65:(j + 1) * 65],
                                              scalar1=COL16[:, h * 16 + (i - j):
                                                            h * 16 + (i - j) + 1],
                                              scalar2=None, op0=ALU.mult).then_inc(s1["pvv"], 1)
                        w.bump(s1["pvv"], 1)
                        w.wait("tensor", s1["pvv"])
                        if w.on("tensor"):
                            eng.matmul(bank[:, 0:65],
                                       Pbuf[:, POFF[j] + (i - j) * 128:
                                            POFF[j] + (i - j) * 128 + 128],
                                       vtl[pvi % 2][:],
                                       start=(j == 0), stop=(j == i)).then_inc(s1["pvm"], 1)
                        w.bump(s1["pvm"], 1)
                    w.wait("vector", s1["pvm"])
                    if w.on("vector"):
                        eng.tensor_scalar(out=odb[:, i * 65:(i + 1) * 65],
                                          in0=bank[:, 0:65],
                                          scalar1=DESC[:, h:h + 1], scalar2=None,
                                          op0=ALU.mult).then_inc(s1["od"], 1)
                    w.bump(s1["od"], 1)
                w.wait("scalar", s1["od"])
                if w.on("scalar"):
                    eng.activation(out=rin16[:],
                                   in_=odb.rearrange("p (i c) -> p i c", c=65)
                                   [:, :, 64:65], func=AF.Ln)
                    eng.drain()
                    eng.activation(out=rin16[:], in_=rin16[:], func=AF.Exp,
                                   scale=-1.0).then_inc(s1["rin"], 1)
                w.bump(s1["rin"], 1)
                pbase = base
                for i in range(TTN):
                    w.wait("vector", s1["rin"])
                    if i >= 1:
                        w.wait("vector", s1["aoT"], w.val(s1["aoT"]))
                    if w.on("vector"):
                        eng.tensor_scalar(out=aot[:, 0:64],
                                          in0=odb[:, i * 65:i * 65 + 64],
                                          scalar1=rin16[:, i:i + 1], scalar2=None,
                                          op0=ALU.mult).then_inc(s1["ao"], 1)
                    w.bump(s1["ao"], 1)
                    w.wait("tensor", s1["ao"])
                    if i >= 1:
                        w.wait("tensor", s1["atc"], w.val(s1["atc"]))
                    if w.on("tensor"):
                        eng.transpose(PSB[0][pbase:pbase + 64, 0:128],
                                      aot[:, 0:64], IDENTB[:]).then_inc(s1["aoT"], 1)
                    w.bump(s1["aoT"], 1)
                    w.wait("vector", s1["aoT"])
                    if w.on("vector"):
                        eng.tensor_copy(
                            out=AT[pbase:pbase + 64,
                                   (h // 2) * T + i * 128:
                                   (h // 2) * T + (i + 1) * 128],
                            in_=PSB[0][pbase:pbase + 64, 0:128]).then_inc(s1["atc"], 1)
                    w.bump(s1["atc"], 1)
                w.wait("tensor", s1["atc"])
                w.wait("scalar", s1["atc"])

        for ck in range(CCN):
            wpp.pre_load()
            if w.on("sync"):
                eng.dma_start(out=wpp.slot(),
                              in_=ext["w_proj"][ck * 128:(ck + 1) * 128, :]
                              ).then_inc(wpp.ld_sem(), 16)
            wpp.did_load()
            wpp.pre_consume()
            ins = None
            if w.on("vector"):
                ins = eng.tensor_copy(out=wP[:, ck * C:(ck + 1) * C],
                                      in_=wpp.cslot())
            wpp.did_consume(ins)
        w.wait("tensor", s1["wfree"])
        for tch in range(4):
            if tch >= 1:
                w.wait("vector", s1["tr2"], w.val(s1["tr2"]))
            for co in range(CCN):
                pband = PSF[co % 2]
                pjidx = w.val(s1["pj"])
                if pjidx >= 2:
                    w.wait("tensor", s1["pje"], pjidx - 1)
                if w.on("tensor"):
                    for ck in range(CCN):
                        ins = eng.matmul(pband[:, 0:512],
                                         wP[:, ck * C + co * 128:
                                            ck * C + (co + 1) * 128],
                                         AT[:, ck * T + tch * 512:
                                            ck * T + (tch + 1) * 512],
                                         start=(ck == 0), stop=(ck == CCN - 1))
                    ins.then_inc(s1["pj"], 1)
                w.bump(s1["pj"], 1)
                w.wait("vector", s1["pj"])
                if w.on("vector"):
                    eng.tensor_scalar(out=yfb[:, co * 512:(co + 1) * 512],
                                      in0=pband[:, 0:512],
                                      scalar1=BPROJ[:, co:co + 1],
                                      scalar2=mc(0, 2, co, b),
                                      op0=ALU.add, op1=ALU.mult).then_inc(s1["pje"], 1)
                w.bump(s1["pje"], 1)
            for st_ in range(4):
                g = b * TTN + tch * 4 + st_
                tt = tch * 4 + st_
                w.wait("tensor", s1["pje"])
                nxo = w.val(s1["xnew"])
                if nxo >= 2 - 1:
                    w.wait("tensor", s1["xnew"], nxo)
                if w.on("tensor"):
                    for co in range(CCN):
                        ins = eng.matmul(PSB[1][:, co * 128:(co + 1) * 128],
                                         yfb[:, co * 512 + st_ * 128:
                                             co * 512 + (st_ + 1) * 128],
                                         IDENTB[:], is_transpose=True,
                                         start=(co == 0), stop=(co == CCN - 1))
                    ins.then_inc(s1["tr2"], 1)
                w.bump(s1["tr2"], 1)
                slot = xcnt[0] % 2
                m = b * TTN + tt
                if m >= 2:
                    w.wait("sync", s1["xnew"], m - 1)
                    xsg = s1["xst"] if slot == 0 else s1["xst2"]
                    w.wait("sync", xsg, w.val(xsg))
                if tt < 2:
                    w.wait("sync", s1["h0"], TTN * b + TTN - 2 + tt + 1)
                if w.on("sync"):
                    eng.dma_start(out=xa[slot][:],
                                  in_=ext["x"][b, tt * 128:(tt + 1) * 128, :]
                                  ).then_inc(xls[slot], 16)
                w.bump(xls[slot], 16)
                xcnt[0] += 1
                w.wait("vector", xls[slot], (xcnt[0] + 1) // 2 * 16)
                xt = xa[slot][:]
                w.wait("vector", s1["tr2"])
                xsts = s1["xst"] if slot == 0 else s1["xst2"]
                if w.on("vector"):
                    eng.tensor_tensor(out=xt, in0=xt,
                                      in1=PSB[1][:, 0:1024], op=ALU.add
                                      ).then_inc(s1["xnew"], 1)
                w.bump(s1["xnew"], 1)
                w.wait("gpsimd", s1["xnew"])
                if w.on("gpsimd"):
                    eng.dma_start(out=sc["X"][b, tt * 128:(tt + 1) * 128, :],
                                  in_=xt).then_inc(xsts, 16)
                    if "X1" in tap_aps:
                        eng.dma_start(
                            out=tap_aps["X1"][b, tt * 128:(tt + 1) * 128, :],
                            in_=xt).then_inc(xsts, 16)
                w.bump(xsts, 16 * ntap)
        for ssem in (s1["xst"], s1["xst2"]):
            w.wait("gpsimd", ssem)
            w.wait("vector", ssem)
            w.wait("sync", ssem)

    add_phase("P1", walk)


def _l0_ffn(S):
    nc, sem, ext, sc, tap_aps = S["nc"], S["sem"], S["ext"], S["sc"], S["tap_aps"]
    CP, mc = S["CP"], S["mc"]
    IDENTB, IDENTF, ONESROW = S["IDENTB"], S["IDENTF"], S["ONESROW"]
    AR2, AR4, ARI, PSF, PSB = S["AR2"], S["AR4"], S["ARI"], S["PSF"], S["PSB"]
    add_phase = S["add_phase"]

    o2 = [0]

    def a2(cols):
        s = o2[0]
        o2[0] += cols
        assert o2[0] <= AR2_COLS, f"AR2 overflow {o2[0]}"
        return AR2[:, s:s + cols]

    o4 = [0]

    def a4(cols):
        s = o4[0]
        o4[0] += cols
        assert o4[0] <= AR4_COLS, f"AR4 overflow {o4[0]}"
        return AR4[:, s:s + cols]

    OC0 = I0 // 128
    wT1 = a2(CCN * I0)
    wT2 = a2(CCN * I0)
    hq2 = a2(CCN * T)
    stg1 = a2(T)
    stg2 = a2(T)
    wq_nat = a2(3072)
    yfh = a2(CCN * 512)

    xa2 = [a4(C), a4(C)]
    wstg2 = [a4(1024), a4(1024)]
    f32a = a4(2048)
    f32b = a4(2048)
    f32c = f32b[:, 0:1024]
    sccol = a4(16)
    mxcol = a4(8)
    lncols2 = a4(8)
    junk42 = f32a[:, 0:C]
    ws = a4(288)

    s2_ = {k: sem(k) for k in
           ("xld", "xld2", "hsp", "wld", "wld2", "wfree", "lnst",
            "lninv", "h0", "trA", "hTr", "mx", "cc", "ccd", "ccc", "rs",
            "dq", "dqt", "dqe", "q2", "mm", "ev", "st", "st2",
            "gld", "gld2", "gst", "gst2", "tr2", "xnew", "xst", "xst2")}

    ntap = 2 if "X2" in tap_aps else 1

    def cchain(w, eng, k, maxsrc_cols, ncols, ready_sem):
        w.wait("tensor", ready_sem)
        if w.on("tensor"):
            eng.matmul(PSF[0][0:ncols, 0:128], maxsrc_cols, IDENTF[:],
                       is_transpose=True, start=True, stop=True
                       ).then_inc(s2_["cc"], 1)
        w.bump(s2_["cc"], 1)
        w.wait("vector", s2_["cc"])
        if w.on("vector"):
            eng.tensor_reduce(out=f32a[0:ncols, 0:1],
                              in_=PSF[0][0:ncols, 0:128],
                              axis=AX.X, op=ALU.max).then_inc(s2_["cc"], 1)
        w.bump(s2_["cc"], 1)
        ccd0 = w.val(s2_["ccd"])
        ccc0 = w.val(s2_["ccc"])
        w.wait("gpsimd", s2_["cc"])
        if w.on("gpsimd"):
            eng.dma_start(out=sc["cc_in"][k][0:1, 0:ncols],
                          in_=f32a[0:ncols, 0:1]).then_inc(s2_["ccd"], 16)
            eng.wait_ge(s2_["ccd"], ccd0 + 16)
            eng.collective_compute(
                "AllReduce", ALU.max,
                replica_groups=[list(range(NC_CC[0]))],
                ins=[sc["cc_in"][k]], outs=[sc["cc_out"][k]],
            ).then_inc(s2_["ccc"], 1)
            eng.wait_ge(s2_["ccc"], ccc0 + 1)
            eng.dma_start(out=f32a[0:1, 8:8 + ncols],
                          in_=sc["cc_out"][k][0:1, 0:ncols]
                          ).then_inc(s2_["ccd"], 16)
        w.bump(s2_["ccd"], 32)
        w.bump(s2_["ccc"], 1)
        w.wait("vector", s2_["ccd"])
        if w.on("vector"):
            eng.tensor_scalar(out=f32a[0:1, 16:16 + ncols],
                              in0=f32a[0:1, 8:8 + ncols],
                              scalar1=1.0 / 32767.0, scalar2=1e-12,
                              op0=ALU.mult, op1=ALU.max).then_inc(s2_["rs"], 1)
        w.bump(s2_["rs"], 1)
        w.wait("scalar", s2_["rs"])
        if w.on("scalar"):
            eng.activation(out=f32a[0:1, 24:24 + ncols],
                           in_=f32a[0:1, 16:16 + ncols], func=AF.Ln)
            eng.drain()
            eng.activation(out=f32a[0:1, 32:32 + ncols],
                           in_=f32a[0:1, 24:24 + ncols],
                           func=AF.Exp, scale=-1.0).then_inc(s2_["rs"], 1)
        w.bump(s2_["rs"], 1)
        w.wait("tensor", s2_["rs"])
        if w.on("tensor"):
            eng.matmul(PSF[0][:, 128:128 + ncols], ONESROW[:],
                       f32a[0:1, 16:16 + ncols], start=True, stop=False)
            eng.matmul(PSF[0][:, 128 + ncols:128 + 2 * ncols], ONESROW[:],
                       f32a[0:1, 32:32 + ncols], start=False, stop=True
                       ).then_inc(s2_["cc"], 1)
        w.bump(s2_["cc"], 1)
        w.wait("vector", s2_["cc"])
        if w.on("vector"):
            ins = eng.tensor_copy(out=sccol[:, 4 * k:4 * k + 2 * ncols],
                                  in_=PSF[0][:, 128:128 + 2 * ncols])
            eng.drain()
            ins.then_inc(s2_["rs"], 1)
        w.bump(s2_["rs"], 1)
        w.wait("tensor", s2_["rs"])
        w.wait("scalar", s2_["rs"])

    def rnd_inplace(w, eng, buf, width):
        if w.on("vector"):
            eng.tensor_scalar(out=buf[:, 0:width], in0=buf[:, 0:width],
                              scalar1=MAGIC, scalar2=MAGIC,
                              op0=ALU.add, op1=ALU.subtract)
            eng.drain()

    def qdq_to(w, eng, src_ap, dst_ap, rcol, scol, width, lo, hi, scratch):
        if w.on("vector"):
            eng.tensor_scalar(out=scratch[:, 0:width], in0=src_ap, scalar1=rcol,
                              scalar2=MAGIC, op0=ALU.mult, op1=ALU.add)
            eng.drain()
            eng.tensor_scalar(out=scratch[:, 0:width], in0=scratch[:, 0:width],
                              scalar1=-MAGIC, scalar2=float(hi),
                              op0=ALU.add, op1=ALU.min)
            eng.drain()
            eng.tensor_scalar(out=dst_ap, in0=scratch[:, 0:width],
                              scalar1=float(lo), scalar2=scol,
                              op0=ALU.max, op1=ALU.mult)
            eng.drain()

    def walk(w):
        eng = w.eng
        wpp = PingPong(w, wstg2, [s2_["wld"], s2_["wld2"]], s2_["wfree"],
                       "sync", "vector")
        lsem = {"st": s2_["lnst"], "inv": s2_["lninv"], "h0": s2_["h0"],
                "trA": s2_["trA"], "hTr": s2_["hTr"]}
        xls = [s2_["xld"], s2_["xld2"]]
        xlb = [w.val(xls[0]), w.val(xls[1])]
        xstb = [w.val(s2_["xst"]), w.val(s2_["xst2"])]
        xnb = w.val(s2_["xnew"])

        if w.on("vector"):
            eng.memset(mxcol[:, 0:8], 0.0).then_inc(s2_["mx"], 1)
        w.bump(s2_["mx"], 1)

        for b in range(B):
            if b > 0:
                w.wait("scalar", s2_["hsp"])
            for tt in range(TTN):
                g = b * TTN + tt
                slot = g % 2
                if g >= 2:
                    w.wait("sync", s2_["h0"], w.val(s2_["h0"]) - 1)
                if w.on("sync"):
                    eng.dma_start(out=xa2[slot][:],
                                  in_=sc["X"][b, tt * 128:(tt + 1) * 128, :]
                                  ).then_inc(xls[slot], 16)
                w.bump(xls[slot], 16)
                w.wait("vector", xls[slot], xlb[slot] + (g // 2 + 1) * 16)
                xt = xa2[slot][:]
                if g >= 2:
                    w.wait("vector", s2_["trA"], w.val(s2_["trA"]) - 1)
                if g >= 2:
                    w.wait("tensor", s2_["hTr"], w.val(s2_["hTr"]) - 1)
                ln_to_hT(w, eng, lsem, xt,
                         (stg1 if slot == 0 else stg2)[:, 0:C],
                         PSB[slot], hq2, tt, lncols2, junk42, IDENTB,
                         sc_col=lambda ck: mc(0, 4, ck, b),
                         bi_col=lambda ck: mc(0, 3, ck, b))
                w.wait("vector", s2_["hTr"])
                if w.on("vector"):
                    eng.tensor_reduce(
                        out=mxcol[:, 5:6],
                        in_=hq2.rearrange("p (cc t) -> p cc t", cc=CCN)
                        [:, :, tt * 128:(tt + 1) * 128],
                        axis=AX.XY, op=ALU.max, apply_absolute_value=True)
                    eng.drain()
                    eng.tensor_tensor(out=mxcol[:, 0:1], in0=mxcol[:, 0:1],
                                      in1=mxcol[:, 5:6], op=ALU.max
                                      ).then_inc(s2_["mx"], 1)
                w.bump(s2_["mx"], 1)
            w.wait("gpsimd", s2_["hTr"])
            if w.on("gpsimd"):
                for ic in range(CCN):
                    eng.dma_start(out=sc["H2T"][b, ic],
                                  in_=hq2[:, ic * T:(ic + 1) * T]
                                  ).then_inc(s2_["hsp"], 16)
            w.bump(s2_["hsp"], 16 * CCN)
        w.wait("sync", s2_["hsp"])
        w.wait("vector", s2_["hsp"])
        cchain(w, eng, 0, mxcol[:, 0:1], 1, s2_["mx"])
        sx, rx = sccol[:, 0:1], sccol[:, 1:2]

        for wi, (wext, wTdst) in enumerate([(ext["f0_up"], wT1),
                                            (ext["f0_gate"], wT2)]):
            for oc in range(OC0):
                wpp.pre_load()
                if w.on("sync"):
                    eng.dma_start(out=wpp.slot(),
                                  in_=wext[oc * 128:(oc + 1) * 128, 0:1024]
                                  ).then_inc(wpp.ld_sem(), 16)
                wpp.did_load()
                wpp.pre_consume()
                wch = wpp.cslot()
                if w.on("vector"):
                    eng.tensor_reduce(
                        out=ws[:, 0:32],
                        in_=wch.rearrange("p (nb b2) -> p nb b2", b2=32),
                        axis=AX.X, op=ALU.max, apply_absolute_value=True)
                    eng.drain()
                    eng.tensor_scalar(out=ws[:, 32:64], in0=ws[:, 0:32],
                                      scalar1=1.0 / 7.0, scalar2=1e-12,
                                      op0=ALU.mult, op1=ALU.max
                                      ).then_inc(s2_["dq"], 1)
                w.bump(s2_["dq"], 1)
                w.wait("scalar", s2_["dq"])
                if w.on("scalar"):
                    eng.activation(out=ws[:, 64:96], in_=ws[:, 32:64],
                                   func=AF.Ln)
                    eng.drain()
                    eng.activation(out=ws[:, 64:96], in_=ws[:, 64:96],
                                   func=AF.Exp, scale=-1.0).then_inc(s2_["dq"], 1)
                w.bump(s2_["dq"], 1)
                w.wait("vector", s2_["dq"])
                ins = None
                if w.on("vector"):
                    eng.tensor_tensor(
                        out=f32c.rearrange("p (nb b2) -> p nb b2", b2=32),
                        in0=wch.rearrange("p (nb b2) -> p nb b2", b2=32),
                        in1=ws[:, 64:96].unsqueeze(-1).broadcast_to([128, 32, 32]),
                        op=ALU.mult)
                    eng.drain()
                rnd_inplace(w, eng, f32c, 1024)
                if w.on("vector"):
                    eng.tensor_scalar(out=f32c, in0=f32c,
                                      scalar1=7.0, scalar2=-8.0,
                                      op0=ALU.min, op1=ALU.max)
                    eng.drain()
                    ins = eng.tensor_tensor(
                        out=wq_nat[:, 0:1024].rearrange(
                            "p (nb b2) -> p nb b2", b2=32),
                        in0=f32c.rearrange("p (nb b2) -> p nb b2", b2=32),
                        in1=ws[:, 32:64].unsqueeze(-1).broadcast_to([128, 32, 32]),
                        op=ALU.mult)
                wpp.did_consume(ins)
                gdq = w.val(s2_["dqt"])
                if gdq >= 1:
                    w.wait("tensor", s2_["dqe"], gdq)
                w.wait("tensor", s2_["wfree"])
                if w.on("tensor"):
                    for ic in range(CCN):
                        ins = eng.matmul(PSB[0][:, ic * 128:(ic + 1) * 128],
                                         wq_nat[:, ic * 128:(ic + 1) * 128],
                                         IDENTB[:], is_transpose=True,
                                         start=(ic == 0), stop=(ic == CCN - 1))
                    ins.then_inc(s2_["dqt"], 1)
                w.bump(s2_["dqt"], 1)
                w.wait("vector", s2_["dqt"])
                if w.on("vector"):
                    for ic in range(CCN):
                        ins = eng.tensor_copy(
                            out=wTdst[:, ic * I0 + oc * 128:
                                      ic * I0 + (oc + 1) * 128],
                            in_=PSB[0][:, ic * 128:(ic + 1) * 128])
                    ins.then_inc(s2_["dqe"], 1)
                w.bump(s2_["dqe"], 1)
        w.wait("tensor", s2_["dqe"])

        for b in range(B):
            if b > 0:
                w.wait("sync", s2_["ev"])
            for ic in range(CCN):
                if w.on("sync"):
                    if b > 0 or ic > 0:
                        eng.wait_ge(s2_["q2"], w.val(s2_["q2"]))
                    eng.dma_start(out=hq2[:, ic * T:(ic + 1) * T],
                                  in_=sc["H2T"][b, ic]).then_inc(s2_["gld"], 16)
                w.bump(s2_["gld"], 16)
                w.wait("vector", s2_["gld"])
                qdq_to(w, eng, hq2[:, ic * T:(ic + 1) * T],
                       hq2[:, ic * T:(ic + 1) * T], rx, sx, T,
                       -32768.0, 32767.0, f32a)
                if w.on("vector"):
                    eng.nop().then_inc(s2_["q2"], 1)
                w.bump(s2_["q2"], 1)
            w.wait("tensor", s2_["q2"])
            for oc in range(OC0):
                for wi, (wTsrc, stg, hbmdst, sts) in enumerate(
                        [(wT1, stg1, sc["UP"], s2_["st"]),
                         (wT2, stg2, sc["GT"], s2_["st2"])]):
                    if b > 0 or oc > 0:
                        w.wait("vector", sts, w.val(sts))
                    for tch in range(4):
                        mmi = w.val(s2_["mm"])
                        bank = PSF[2 + mmi % 2]
                        if mmi >= 2:
                            w.wait("tensor", s2_["ev"], mmi - 1)
                        if w.on("tensor"):
                            for ic in range(CCN):
                                ins = eng.matmul(
                                    bank[:, 0:512],
                                    wTsrc[:, ic * I0 + oc * 128:
                                          ic * I0 + (oc + 1) * 128],
                                    hq2[:, ic * T + tch * 512:
                                        ic * T + (tch + 1) * 512],
                                    start=(ic == 0), stop=(ic == CCN - 1))
                            ins.then_inc(s2_["mm"], 1)
                        w.bump(s2_["mm"], 1)
                        w.wait("vector", s2_["mm"])
                        if w.on("vector"):
                            eng.drain()
                            eng.tensor_copy(out=stg[:, tch * 512:(tch + 1) * 512],
                                            in_=bank[:, 0:512])
                            eng.tensor_reduce(out=mxcol[:, 5:6],
                                              in_=bank[:, 0:512], axis=AX.X,
                                              op=ALU.max,
                                              apply_absolute_value=True)
                            eng.drain()
                            eng.tensor_tensor(out=mxcol[:, 1 + wi:2 + wi],
                                              in0=mxcol[:, 1 + wi:2 + wi],
                                              in1=mxcol[:, 5:6],
                                              op=ALU.max).then_inc(s2_["ev"], 1)
                        w.bump(s2_["ev"], 1)
                    w.wait("gpsimd", s2_["ev"])
                    if w.on("gpsimd"):
                        eng.dma_start(out=hbmdst[b, oc], in_=stg[:]
                                      ).then_inc(sts, 16)
                    w.bump(sts, 16)
        w.wait("vector", s2_["st"])
        w.wait("vector", s2_["st2"])
        w.wait("sync", s2_["st"])
        w.wait("sync", s2_["st2"])
        cchain(w, eng, 1, mxcol[:, 1:3], 2, s2_["ev"])
        s_up, s_g = sccol[:, 4:5], sccol[:, 5:6]
        r_up, r_g = sccol[:, 6:7], sccol[:, 7:8]

        for b in range(B):
            for oc in range(OC0):
                m = b * OC0 + oc
                slot = m % 2
                gls = s2_["gld"] if slot == 0 else s2_["gld2"]
                gss = s2_["gst"] if slot == 0 else s2_["gst2"]
                if m >= 2:
                    w.wait("sync", gss, w.val(gss))
                if w.on("sync"):
                    eng.dma_start(out=(stg1 if slot == 0 else stg2)[:],
                                  in_=sc["GT"][b, oc]).then_inc(gls, 16)
                w.bump(gls, 16)
                w.wait("vector", gls)
                gsrc = (stg1 if slot == 0 else stg2)[:]
                if w.on("vector"):
                    eng.tensor_scalar(out=f32a[:, 0:T], in0=gsrc, scalar1=r_g,
                                      scalar2=None, op0=ALU.mult)
                    eng.drain()
                rnd_inplace(w, eng, f32a, T)
                if w.on("vector"):
                    eng.tensor_scalar(out=f32a[:, 0:T], in0=f32a[:, 0:T],
                                      scalar1=32767.0, scalar2=-32768.0,
                                      op0=ALU.min, op1=ALU.max)
                    eng.drain()
                    eng.nop().then_inc(s2_["dq"], 1)
                w.bump(s2_["dq"], 1)
                w.wait("scalar", s2_["dq"])
                if w.on("scalar"):
                    eng.activation(out=f32b[:, 0:T], in_=f32a[:, 0:T],
                                   func=AF.Sigmoid, scale=s_g
                                   ).then_inc(s2_["dq"], 1)
                w.bump(s2_["dq"], 1)
                w.wait("vector", s2_["dq"])
                ins = None
                if w.on("vector"):
                    eng.tensor_scalar(out=f32b[:, 0:T], in0=f32b[:, 0:T],
                                      scalar1=65536.0, scalar2=None,
                                      op0=ALU.mult)
                    eng.drain()
                rnd_inplace(w, eng, f32b, T)
                if w.on("vector"):
                    eng.tensor_scalar(out=f32b[:, 0:T], in0=f32b[:, 0:T],
                                      scalar1=65535.0, scalar2=0.0,
                                      op0=ALU.min, op1=ALU.max)
                    eng.drain()
                    eng.tensor_scalar(out=mxcol[:, 6:7], in0=s_g,
                                      scalar1=1.0 / 65536.0, scalar2=None,
                                      op0=ALU.mult)
                    eng.drain()
                    eng.tensor_scalar(out=f32a[:, 0:T], in0=f32a[:, 0:T],
                                      scalar1=mxcol[:, 6:7], scalar2=None,
                                      op0=ALU.mult)
                    eng.drain()
                    eng.tensor_tensor(out=gsrc, in0=f32a[:, 0:T],
                                      in1=f32b[:, 0:T], op=ALU.mult)
                    eng.drain()
                    eng.tensor_reduce(out=mxcol[:, 5:6], in_=gsrc,
                                      axis=AX.X, op=ALU.max,
                                      apply_absolute_value=True)
                    eng.drain()
                    ins = eng.tensor_tensor(out=mxcol[:, 3:4], in0=mxcol[:, 3:4],
                                            in1=mxcol[:, 5:6], op=ALU.max)
                    ins.then_inc(s2_["q2"], 1)
                w.bump(s2_["q2"], 1)
                w.wait("gpsimd", s2_["q2"])
                if w.on("gpsimd"):
                    eng.dma_start(out=sc["G2"][b, oc], in_=gsrc
                                  ).then_inc(gss, 16)
                w.bump(gss, 16)
        w.wait("sync", s2_["gst"])
        w.wait("sync", s2_["gst2"])
        w.wait("vector", s2_["gst"])
        w.wait("vector", s2_["gst2"])
        cchain(w, eng, 2, mxcol[:, 3:4], 1, s2_["q2"])
        s_g2, r_g2 = sccol[:, 8:9], sccol[:, 9:10]

        for b in range(B):
            for oc in range(OC0):
                m = b * OC0 + oc
                slot = m % 2
                gls = s2_["gld"] if slot == 0 else s2_["gld2"]
                gss = s2_["gst"] if slot == 0 else s2_["gst2"]
                if m >= 2:
                    w.wait("sync", gss, w.val(gss))
                if m >= 4:
                    w.wait("sync", s2_["q2"], w.val(s2_["q2"]) - 3)
                if w.on("sync"):
                    eng.dma_start(out=(stg1 if slot == 0 else stg2)[:],
                                  in_=sc["G2"][b, oc]).then_inc(gls, 16)
                    eng.dma_start(out=hq2[:, (m % 4) * T:(m % 4 + 1) * T],
                                  in_=sc["UP"][b, oc]).then_inc(gls, 16)
                w.bump(gls, 32)
                w.wait("vector", gls)
                gsrc = (stg1 if slot == 0 else stg2)[:]
                upsrc = hq2[:, (m % 4) * T:(m % 4 + 1) * T]
                qdq_to(w, eng, gsrc, f32b[:, 0:T], r_g2,
                       s_g2, T, -32768.0, 32767.0, f32a)
                if w.on("vector"):
                    eng.tensor_copy(out=f32a[:, 0:T], in_=f32b[:, 0:T])
                    eng.drain()
                qdq_to(w, eng, upsrc, f32b[:, 0:T], r_up,
                       s_up, T, -32768.0, 32767.0, f32b)
                ins = None
                if w.on("vector"):
                    eng.tensor_tensor(out=gsrc, in0=f32a[:, 0:T],
                                      in1=f32b[:, 0:T], op=ALU.mult)
                    eng.drain()
                    eng.tensor_reduce(out=mxcol[:, 5:6], in_=gsrc,
                                      axis=AX.X, op=ALU.max,
                                      apply_absolute_value=True)
                    eng.drain()
                    ins = eng.tensor_tensor(out=mxcol[:, 4:5], in0=mxcol[:, 4:5],
                                            in1=mxcol[:, 5:6], op=ALU.max)
                    ins.then_inc(s2_["q2"], 1)
                w.bump(s2_["q2"], 1)
                w.wait("gpsimd", s2_["q2"])
                if w.on("gpsimd"):
                    eng.dma_start(out=sc["TO"][b, oc], in_=gsrc
                                  ).then_inc(gss, 16)
                w.bump(gss, 16)
        w.wait("sync", s2_["gst"])
        w.wait("sync", s2_["gst2"])
        w.wait("vector", s2_["gst"])
        w.wait("vector", s2_["gst2"])
        cchain(w, eng, 3, mxcol[:, 4:5], 1, s2_["q2"])
        s_o, r_o = sccol[:, 12:13], sccol[:, 13:14]

        w.wait("tensor", s2_["ev"])
        for cc in range(CCN):
            for pc in range(3):
                wpp.pre_load()
                if w.on("sync"):
                    eng.dma_start(out=wpp.slot(),
                                  in_=ext["f0_down"][cc * 128:(cc + 1) * 128,
                                                     pc * 1024:(pc + 1) * 1024]
                                  ).then_inc(wpp.ld_sem(), 16)
                wpp.did_load()
                wpp.pre_consume()
                wch = wpp.cslot()
                if w.on("vector"):
                    eng.tensor_reduce(
                        out=ws[:, 0:32],
                        in_=wch.rearrange("p (nb b2) -> p nb b2", b2=32),
                        axis=AX.X, op=ALU.max, apply_absolute_value=True)
                    eng.drain()
                    eng.tensor_scalar(out=ws[:, 32:64], in0=ws[:, 0:32],
                                      scalar1=1.0 / 7.0, scalar2=1e-12,
                                      op0=ALU.mult, op1=ALU.max
                                      ).then_inc(s2_["dq"], 1)
                w.bump(s2_["dq"], 1)
                w.wait("scalar", s2_["dq"])
                if w.on("scalar"):
                    eng.activation(out=ws[:, 64:96], in_=ws[:, 32:64],
                                   func=AF.Ln)
                    eng.drain()
                    eng.activation(out=ws[:, 64:96], in_=ws[:, 64:96],
                                   func=AF.Exp, scale=-1.0).then_inc(s2_["dq"], 1)
                w.bump(s2_["dq"], 1)
                w.wait("vector", s2_["dq"])
                ins = None
                if w.on("vector"):
                    eng.tensor_tensor(
                        out=f32c.rearrange("p (nb b2) -> p nb b2", b2=32),
                        in0=wch.rearrange("p (nb b2) -> p nb b2", b2=32),
                        in1=ws[:, 64:96].unsqueeze(-1).broadcast_to([128, 32, 32]),
                        op=ALU.mult)
                    eng.drain()
                rnd_inplace(w, eng, f32c, 1024)
                if w.on("vector"):
                    eng.tensor_scalar(out=f32c, in0=f32c,
                                      scalar1=7.0, scalar2=-8.0,
                                      op0=ALU.min, op1=ALU.max)
                    eng.drain()
                    ins = eng.tensor_tensor(
                        out=wq_nat[:, pc * 1024:(pc + 1) * 1024].rearrange(
                            "p (nb b2) -> p nb b2", b2=32),
                        in0=f32c.rearrange("p (nb b2) -> p nb b2", b2=32),
                        in1=ws[:, 32:64].unsqueeze(-1).broadcast_to([128, 32, 32]),
                        op=ALU.mult)
                wpp.did_consume(ins)
            w.wait("tensor", s2_["wfree"])
            for g8 in range(3):
                w.wait("tensor", s2_["dqe"])
                if w.on("tensor"):
                    for ot in range(8):
                        ins = eng.matmul(
                            PSB[0][:, ot * 128:(ot + 1) * 128],
                            wq_nat[:, (g8 * 8 + ot) * 128:(g8 * 8 + ot + 1) * 128],
                            IDENTB[:], is_transpose=True,
                            start=(ot == 0), stop=(ot == 7))
                    ins.then_inc(s2_["dqt"], 1)
                w.bump(s2_["dqt"], 1)
                w.wait("vector", s2_["dqt"])
                if w.on("vector"):
                    for ot in range(8):
                        oc = g8 * 8 + ot
                        ins = eng.tensor_copy(
                            out=wT1[:, oc * C + cc * 128:oc * C + (cc + 1) * 128],
                            in_=PSB[0][:, ot * 128:(ot + 1) * 128])
                    ins.then_inc(s2_["dqe"], 1)
                w.bump(s2_["dqe"], 1)
        w.wait("tensor", s2_["dqe"])

        for b in range(B):
            for th in range(2):
                w.wait("vector", s2_["mm"])
                for oc in range(OC0):
                    m = w.val(s2_["gld"]) // 16
                    slot = m % 2
                    gls = s2_["gld"] if slot == 0 else s2_["gld2"]
                    if w.on("sync"):
                        eng.wait_ge(s2_["q2"], w.val(s2_["q2"]))
                        pass
                    if w.on("sync"):
                        eng.dma_start(
                            out=(stg1 if slot == 0 else stg2)[:, 0:1024],
                            in_=sc["TO"][b, oc][:, th * 1024:(th + 1) * 1024]
                            ).then_inc(gls, 16)
                    w.bump(gls, 16)
                    w.wait("vector", gls)
                    qdq_to(w, eng, (stg1 if slot == 0 else stg2)[:, 0:1024],
                           wT2[:, oc * 1024:(oc + 1) * 1024], r_o, s_o,
                           1024, -32768.0, 32767.0, f32a)
                    if w.on("vector"):
                        eng.nop().then_inc(s2_["q2"], 1)
                    w.bump(s2_["q2"], 1)
                w.wait("tensor", s2_["q2"])
                for tch in range(2):
                    for cc in range(CCN):
                        mmi = w.val(s2_["mm"])
                        bank = PSF[2 + mmi % 2]
                        if mmi >= 2:
                            w.wait("tensor", s2_["ev"], mmi - 1)
                        if w.on("tensor"):
                            for oc in range(OC0):
                                ins = eng.matmul(
                                    bank[:, 0:512],
                                    wT1[:, oc * C + cc * 128:
                                        oc * C + (cc + 1) * 128],
                                    wT2[:, oc * 1024 + tch * 512:
                                        oc * 1024 + (tch + 1) * 512],
                                    start=(oc == 0), stop=(oc == OC0 - 1))
                            ins.then_inc(s2_["mm"], 1)
                        w.bump(s2_["mm"], 1)
                        w.wait("vector", s2_["mm"])
                        if w.on("vector"):
                            eng.tensor_scalar(out=yfh[:, cc * 512:(cc + 1) * 512],
                                              in0=bank[:, 0:512],
                                              scalar1=mc(0, 5, cc, b),
                                              scalar2=None, op0=ALU.mult
                                              ).then_inc(s2_["ev"], 1)
                        w.bump(s2_["ev"], 1)
                    for st_ in range(4):
                        tt = th * 8 + tch * 4 + st_
                        gx = w.val(s2_["xnew"])
                        w.wait("tensor", s2_["ev"])
                        if gx >= 1:
                            w.wait("tensor", s2_["xnew"], gx)
                        if w.on("tensor"):
                            for cc in range(CCN):
                                ins = eng.matmul(
                                    PSB[1][:, cc * 128:(cc + 1) * 128],
                                    yfh[:, cc * 512 + st_ * 128:
                                        cc * 512 + (st_ + 1) * 128],
                                    IDENTB[:], is_transpose=True,
                                    start=(cc == 0), stop=(cc == CCN - 1))
                            ins.then_inc(s2_["tr2"], 1)
                        w.bump(s2_["tr2"], 1)
                        xslot = gx % 2
                        xsl = xls[xslot]
                        xsts = s2_["xst"] if xslot == 0 else s2_["xst2"]
                        if gx - xnb >= 2:
                            w.wait("sync", s2_["xnew"], gx - 1)
                            w.wait("sync", xsts,
                                   xstb[xslot] + ((gx - xnb) // 2) * 16 * ntap)
                        if w.on("sync"):
                            eng.dma_start(out=xa2[xslot][:],
                                          in_=sc["X"][b, tt * 128:(tt + 1) * 128, :]
                                          ).then_inc(xsl, 16)
                        w.bump(xsl, 16)
                        w.wait("vector", xsl)
                        w.wait("vector", s2_["tr2"])
                        if w.on("vector"):
                            eng.tensor_tensor(out=xa2[xslot][:], in0=xa2[xslot][:],
                                              in1=PSB[1][:, 0:1024], op=ALU.add
                                              ).then_inc(s2_["xnew"], 1)
                        w.bump(s2_["xnew"], 1)
                        w.wait("gpsimd", s2_["xnew"])
                        if w.on("gpsimd"):
                            eng.dma_start(out=sc["X"][b, tt * 128:(tt + 1) * 128, :],
                                          in_=xa2[xslot][:]).then_inc(xsts, 16)
                            if "X2" in tap_aps:
                                eng.dma_start(
                                    out=tap_aps["X2"][b, tt * 128:(tt + 1) * 128, :],
                                    in_=xa2[xslot][:]).then_inc(xsts, 16)
                        w.bump(xsts, 16 * ntap)
        for ssem in (s2_["xst"], s2_["xst2"]):
            w.wait("gpsimd", ssem)
            w.wait("vector", ssem)
            w.wait("sync", ssem)
        return w

    add_phase("P2", walk)


def _l1_attn(S):
    nc, sem, ext, sc, tap_aps = S["nc"], S["sem"], S["ext"], S["sc"], S["tap_aps"]
    CP, mc = S["CP"], S["mc"]
    IDENTB, IDENTF, ONESROW = S["IDENTB"], S["IDENTF"], S["ONESROW"]
    MASKLT = S["MASKLT"]
    SEL2, SEL2B = S["SEL2"], S["SEL2B"]
    SELCLI, SELCLI2 = S["SELCLI"], S["SELCLI2"]
    AR2, AR4, PSF, PSB = S["AR2"], S["AR4"], S["PSF"], S["PSB"]
    add_phase = S["add_phase"]

    o2 = [0]

    def a2(cols):
        s = o2[0]
        o2[0] += cols
        assert o2[0] <= AR2_COLS, f"AR2 overflow {o2[0]}"
        return AR2[:, s:s + cols]

    o4 = [0]

    def a4(cols):
        s = o4[0]
        o4[0] += cols
        assert o4[0] <= AR4_COLS, f"AR4 overflow {o4[0]}"
        return AR4[:, s:s + cols]

    hT = a2(CCN * T)
    KTf = a2(CCN * T)
    QTf = a2(CCN * T)
    AT = a2(CCN * T)
    wpb = a2(CCN * C)
    yb0 = o2[0]
    vtm = a2(TTN * 2 * D)
    stgW = a2(2048)
    yfb = AR2[:, yb0:yb0 + 4096]
    wbb = a2(128)
    h0b3 = [stgW[:, 0:C], stgW[:, C:2 * C]]
    Wblk2 = [stgW[:, 0:512], stgW[:, 512:1024]]

    xa3 = a4(C)
    wstg3 = [xa3, a4(C)]
    Ebuf2 = [a4(512), a4(512)]
    R1 = a4(2048)
    R2 = a4(2048)
    Rbeta = R1[32:48, 0:T]
    Rcl_cl = R2[32:48, 0:T]
    cli = R1[0:2, 0:T]
    cli2 = R2[0:2, 0:T]
    betaT = a4(256)
    term = a4(128)
    zrow = a4(512)
    lnc3 = a4(8)
    junk43 = R2[:, 0:C]

    _s3map = {"bmm": "sgrp", "bev": "sexp", "bch": "smsk", "btr": "ao",
              "kq": "qk", "kqe": "qke", "nrm": "aoT", "nev": "atc",
              "cl": "rin", "clb": "pvv", "clbe": "od", "smm": "bv",
              "rmm": "mm", "ee": "ev", "wb": "dq", "pve": "dqe"}
    s3_ = {k: sem(_s3map.get(k, k)) for k in
           ("xld", "wld", "wld2", "wfree", "lnst", "lninv", "h0", "trA", "hTr",
            "bmm", "bev", "bch", "btr", "kq", "kqe", "nrm", "nev", "cl",
            "clb", "clbe", "vt", "vte", "smm", "rmm", "ee", "wb",
            "pvm", "pve", "pj", "pje", "tr2", "xnew", "xst")}

    ntap = 2 if "X3" in tap_aps else 1

    def walk(w):
        eng = w.eng
        lsem = {"st": s3_["lnst"], "inv": s3_["lninv"], "h0": s3_["h0"],
                "trA": s3_["trA"], "hTr": s3_["hTr"]}
        if w.on("vector"):
            eng.memset(term[:], 0.0)
            eng.memset(zrow[:], 0.0)

        def load_w(wext):
            for ck in range(CCN):
                nl = (w.val(s3_["wld"]) + w.val(s3_["wld2"])) // 16
                slot = nl % 2
                lsm = s3_["wld"] if slot == 0 else s3_["wld2"]
                if nl >= 2:
                    w.wait("sync", s3_["wfree"], nl - 1)
                if slot == 0:
                    w.wait("sync", s3_["h0"], w.val(s3_["h0"]))
                if w.on("sync"):
                    eng.dma_start(out=wstg3[slot][:],
                                  in_=wext[ck * 128:(ck + 1) * 128, :]
                                  ).then_inc(lsm, 16)
                w.bump(lsm, 16)
                w.wait("vector", lsm)
                ins = None
                if w.on("vector"):
                    ins = eng.tensor_copy(out=wpb[:, ck * C:(ck + 1) * C],
                                          in_=wstg3[slot][:])
                    ins.then_inc(s3_["wfree"], 1)
                w.bump(s3_["wfree"], 1)

        for b in range(B):
            for tt in range(TTN):
                g = b * TTN + tt
                if g >= 1:
                    w.wait("sync", s3_["h0"], w.val(s3_["h0"]))
                if b > 0 and tt == 0:
                    w.wait("sync", s3_["xst"], w.val(s3_["xst"]))
                    w.wait("sync", s3_["wfree"],
                           (w.val(s3_["wld"]) + w.val(s3_["wld2"])) // 16)
                if w.on("sync"):
                    eng.dma_start(out=xa3[:],
                                  in_=sc["X"][b, tt * 128:(tt + 1) * 128, :]
                                  ).then_inc(s3_["xld"], 16)
                w.bump(s3_["xld"], 16)
                w.wait("vector", s3_["xld"])
                if g >= 2:
                    w.wait("vector", s3_["trA"], w.val(s3_["trA"]) - 1)
                    w.wait("tensor", s3_["hTr"], w.val(s3_["hTr"]) - 1)
                ln_to_hT(w, eng, lsem, xa3[:], h0b3[g % 2], PSB[g % 2], hT, tt,
                         lnc3, junk43, IDENTB,
                         sc_col=lambda ck: mc(1, 1, ck, b),
                         bi_col=lambda ck: mc(1, 0, ck, b))

            nlb = (w.val(s3_["wld"]) + w.val(s3_["wld2"])) // 16
            bslot = nlb % 2
            blsm = s3_["wld"] if bslot == 0 else s3_["wld2"]
            if nlb >= 2:
                w.wait("sync", s3_["wfree"], nlb - 1)
            if bslot == 0:
                w.wait("sync", s3_["h0"], w.val(s3_["h0"]))
            if w.on("sync"):
                eng.dma_start(out=wstg3[bslot][:, 0:128].rearrange(
                                  "p (c h) -> p c h", c=CCN),
                              in_=ext["wbeta1"].rearrange("(c p) h -> p c h",
                                                          p=128)
                              ).then_inc(blsm, 16)
            w.bump(blsm, 16)
            w.wait("vector", blsm)
            if w.on("vector"):
                eng.tensor_copy(out=wbb[:], in_=wstg3[bslot][:, 0:128]
                                ).then_inc(s3_["wfree"], 1)
            w.bump(s3_["wfree"], 1)
            w.wait("tensor", s3_["wfree"])
            w.wait("tensor", s3_["hTr"])
            for tch in range(4):
                if tch >= 1:
                    w.wait("tensor", s3_["bev"])
                if w.on("tensor"):
                    for ck in range(CCN):
                        ins = eng.matmul(
                            PSF[0][32:48, 0:512],
                            wbb[:, ck * 16:(ck + 1) * 16],
                            hT[:, ck * T + tch * 512:ck * T + (tch + 1) * 512],
                            start=(ck == 0), stop=(ck == CCN - 1))
                    ins.then_inc(s3_["bmm"], 1)
                w.bump(s3_["bmm"], 1)
                w.wait("scalar", s3_["bmm"])
                if w.on("scalar"):
                    eng.activation(out=Rbeta[:, tch * 512:(tch + 1) * 512],
                                   in_=PSF[0][32:48, 0:512], func=AF.Sigmoid
                                   ).then_inc(s3_["bev"], 1)
                w.bump(s3_["bev"], 1)
            w.wait("tensor", s3_["bev"])
            if w.on("tensor"):
                for i in range(TTN):
                    ins = eng.matmul(PSF[2][:, i * 16:(i + 1) * 16],
                                     Rbeta[:, i * 128:(i + 1) * 128],
                                     IDENTF[32:48, 32:48], is_transpose=True,
                                     start=(i == 0), stop=(i == TTN - 1))
                ins.then_inc(s3_["btr"], 1)
            w.bump(s3_["btr"], 1)
            w.wait("vector", s3_["btr"])
            if w.on("vector"):
                eng.tensor_copy(out=betaT[:], in_=PSF[2][:, 0:256]
                                ).then_inc(s3_["btr"], 1)
            w.bump(s3_["btr"], 1)
            w.wait("scalar", s3_["btr"])
            if w.on("scalar"):
                eng.activation(out=Rbeta[:], in_=Rbeta[:],
                               func=AF.Identity, scale=-1.0,
                               bias=CP[32:48, CP_ONE:CP_ONE + 1]
                               ).then_inc(s3_["bch"], 1)
            w.bump(s3_["bch"], 1)
            w.wait("vector", s3_["bch"])
            if w.on("vector"):
                eng.tensor_scalar(out=Rbeta[:], in0=Rbeta[:],
                                  scalar1=1e-8, scalar2=None, op0=ALU.add
                                  ).then_inc(s3_["bch"], 1)
            w.bump(s3_["bch"], 1)
            w.wait("scalar", s3_["bch"])
            if w.on("scalar"):
                eng.activation(out=Rbeta[:], in_=Rbeta[:],
                               func=AF.Ln).then_inc(s3_["bch"], 1)
            w.bump(s3_["bch"], 1)
            w.wait("vector", s3_["bch"])
            if w.on("vector"):
                eng.memset(R2[32:64, 0:T], 1.0)
                eng.drain()
                eng.tensor_tensor_scan(out=Rcl_cl[:],
                                       data0=Rbeta[:], data1=Rbeta[:],
                                       initial=0.0, op0=ALU.add, op1=ALU.bypass
                                       ).then_inc(s3_["cl"], 1)
            w.bump(s3_["cl"], 1)

            for wwi, (wext, dstT, qsc) in enumerate(
                    [(ext["wk1"], KTf, 1.0), (ext["wq1"], QTf, 0.125)]):
                load_w(wext)
                w.wait("tensor", s3_["wfree"])
                for ec in range(CCN):
                    for tch in range(4):
                        mmi = w.val(s3_["kq"])
                        bank = PSF[mmi % 2]
                        if mmi >= 2:
                            w.wait("tensor", s3_["kqe"], mmi - 1)
                        if w.on("tensor"):
                            for ck in range(CCN):
                                ins = eng.matmul(
                                    bank[:, 0:512],
                                    wpb[:, ck * C + ec * 128:
                                        ck * C + (ec + 1) * 128],
                                    hT[:, ck * T + tch * 512:
                                       ck * T + (tch + 1) * 512],
                                    start=(ck == 0), stop=(ck == CCN - 1))
                            ins.then_inc(s3_["kq"], 1)
                        w.bump(s3_["kq"], 1)
                        w.wait("vector", s3_["kq"])
                        if w.on("vector"):
                            eng.tensor_copy(
                                out=dstT[:, ec * T + tch * 512:
                                         ec * T + (tch + 1) * 512],
                                in_=bank[:, 0:512]).then_inc(s3_["kqe"], 1)
                        w.bump(s3_["kqe"], 1)
                    w.wait("vector", s3_["kqe"])
                    if w.on("vector"):
                        eng.drain()
                        eng.tensor_tensor(
                            out=stgW[:, 0:T],
                            in0=dstT[:, ec * T:(ec + 1) * T],
                            in1=dstT[:, ec * T:(ec + 1) * T], op=ALU.mult
                            ).then_inc(s3_["nrm"], 1)
                    w.bump(s3_["nrm"], 1)
                    for tch in range(4):
                        w.wait("tensor", s3_["nrm"])
                        if tch >= 1:
                            w.wait("tensor", s3_["nev"])
                        if w.on("tensor"):
                            eng.matmul(PSF[2][0:2, 0:512], SEL2[:, 0:2],
                                       stgW[:, tch * 512:(tch + 1) * 512],
                                       start=True, stop=True
                                       ).then_inc(s3_["smm"], 1)
                        w.bump(s3_["smm"], 1)
                        w.wait("vector", s3_["smm"])
                        if w.on("vector"):
                            eng.tensor_copy(
                                out=cli2[0:2, tch * 512:(tch + 1) * 512],
                                in_=PSF[2][0:2, 0:512]).then_inc(s3_["nev"], 1)
                        w.bump(s3_["nev"], 1)
                    w.wait("scalar", s3_["nev"])
                    if w.on("scalar"):
                        eng.activation(out=cli2[0:2, 0:T], in_=cli2[0:2, 0:T],
                                       func=AF.Ln)
                        eng.drain()
                        eng.activation(out=cli2[0:2, 0:T], in_=cli2[0:2, 0:T],
                                       func=AF.Exp, scale=0.5
                                       ).then_inc(s3_["nrm"], 1)
                    w.bump(s3_["nrm"], 1)
                    w.wait("vector", s3_["nrm"])
                    if w.on("vector"):
                        eng.tensor_scalar(out=cli2[0:2, 0:T],
                                          in0=cli2[0:2, 0:T],
                                          scalar1=1e-12, scalar2=None,
                                          op0=ALU.max).then_inc(s3_["nrm"], 1)
                    w.bump(s3_["nrm"], 1)
                    w.wait("scalar", s3_["nrm"])
                    if w.on("scalar"):
                        eng.activation(out=cli2[0:2, 0:T], in_=cli2[0:2, 0:T],
                                       func=AF.Ln)
                        eng.drain()
                        eng.activation(out=cli2[0:2, 0:T], in_=cli2[0:2, 0:T],
                                       func=AF.Exp, scale=-1.0
                                       ).then_inc(s3_["nrm"], 1)
                    w.bump(s3_["nrm"], 1)
                    w.wait("vector", s3_["nrm"])
                    if qsc != 1.0:
                        if w.on("vector"):
                            eng.tensor_scalar(out=cli2[0:2, 0:T],
                                              in0=cli2[0:2, 0:T],
                                              scalar1=qsc, scalar2=None,
                                              op0=ALU.mult)
                            eng.drain().then_inc(s3_["nrm"], 1)
                        w.bump(s3_["nrm"], 1)
                    else:
                        if w.on("vector"):
                            eng.drain().then_inc(s3_["nrm"], 1)
                        w.bump(s3_["nrm"], 1)
                    for tch in range(4):
                        w.wait("tensor", s3_["nrm"])
                        if w.val(s3_["nev"]) >= 1:
                            w.wait("tensor", s3_["nev"])
                        if w.on("tensor"):
                            eng.matmul(PSF[2][:, 0:512], SEL2B[0:2, 0:128],
                                       cli2[0:2, tch * 512:(tch + 1) * 512],
                                       start=True, stop=True
                                       ).then_inc(s3_["smm"], 1)
                        w.bump(s3_["smm"], 1)
                        w.wait("vector", s3_["smm"])
                        if w.on("vector"):
                            eng.tensor_tensor(
                                out=dstT[:, ec * T + tch * 512:
                                         ec * T + (tch + 1) * 512],
                                in0=dstT[:, ec * T + tch * 512:
                                         ec * T + (tch + 1) * 512],
                                in1=PSF[2][:, 0:512], op=ALU.mult
                                ).then_inc(s3_["nev"], 1)
                        w.bump(s3_["nev"], 1)
                    w.wait("tensor", s3_["nev"])

            load_w(ext["wv1"])
            w.wait("tensor", s3_["wfree"])
            w.wait("tensor", s3_["kqe"])
            w.wait("tensor", s3_["nev"])
            for h in range(H):
                ec, bh = h // 2, (h % 2) * 64
                w.wait("tensor", s3_["vte"])
                w.wait("tensor", s3_["wb"], w.val(s3_["wb"]))
                w.wait("tensor", s3_["ee"], w.val(s3_["ee"]))
                for tch in range(4):
                    w.wait("tensor", s3_["cl"])
                    w.wait("tensor", s3_["clbe"])
                    if w.on("tensor"):
                        eng.matmul(PSF[2][0:2, 0:512],
                                   SELCLI[32:49, 2 * h:2 * h + 2],
                                   R2[32:49, tch * 512:(tch + 1) * 512],
                                   start=True, stop=True)
                        eng.matmul(PSF[4][0:2, 0:512],
                                   SELCLI2[32:49, 2 * h:2 * h + 2],
                                   R2[32:49, tch * 512:(tch + 1) * 512],
                                   start=True, stop=True
                                   ).then_inc(s3_["clb"], 1)
                    w.bump(s3_["clb"], 1)
                    w.wait("vector", s3_["clb"])
                    if w.on("vector"):
                        eng.tensor_copy(out=cli[0:2, tch * 512:(tch + 1) * 512],
                                        in_=PSF[2][0:2, 0:512])
                        eng.tensor_copy(out=cli2[0:2, tch * 512:(tch + 1) * 512],
                                        in_=PSF[4][0:2, 0:512]
                                        ).then_inc(s3_["clbe"], 1)
                    w.bump(s3_["clbe"], 1)
                w.wait("tensor", s3_["clbe"])
                if h % 2 == 0:
                    for i in range(TTN):
                        bank = PSF[4 + i % 2]
                        if i >= 2:
                            w.wait("tensor", s3_["vte"],
                                   w.val(s3_["vte"]) - 1)
                        if w.on("tensor"):
                            for ck in range(CCN):
                                ins = eng.matmul(
                                    bank[:, 0:128],
                                    hT[:, ck * T + i * 128:
                                       ck * T + (i + 1) * 128],
                                    wpb[:, ck * C + h * 64:
                                        ck * C + (h + 2) * 64],
                                    start=(ck == 0), stop=(ck == CCN - 1))
                            ins.then_inc(s3_["vt"], 1)
                        w.bump(s3_["vt"], 1)
                        w.wait("vector", s3_["vt"])
                        if w.on("vector"):
                            eng.tensor_copy(out=vtm[:, i * 128:(i + 1) * 128],
                                            in_=bank[:, 0:128]
                                            ).then_inc(s3_["vte"], 1)
                        w.bump(s3_["vte"], 1)
                w.wait("tensor", s3_["vte"], w.val(s3_["vte"]))
                for gi in range(4):
                    w.wait("tensor", s3_["pve"], w.val(s3_["pve"]))
                    if w.on("tensor"):
                        eng.matmul(PSF[3][:, 0:512], zrow[0:1, 0:128],
                                   zrow[0:1, 0:512], start=True, stop=False)
                    for j in range(4 * gi, TTN):
                        sbank = (PSF[0], PSF[2])[j % 2]
                        rbank = (PSF[1], PSF[4])[j % 2]
                        ebuf = Ebuf2[j % 2]
                        w.wait("tensor", s3_["wb"], w.val(s3_["wb"]) - 1)
                        if w.on("tensor"):
                            eng.matmul(sbank[:, 0:512],
                                       KTf[bh:bh + 64,
                                           ec * T + j * 128:ec * T + (j + 1) * 128],
                                       QTf[bh:bh + 64,
                                           ec * T + gi * 512:ec * T + (gi + 1) * 512],
                                       start=True, stop=True
                                       ).then_inc(s3_["smm"], 1)
                        w.bump(s3_["smm"], 1)
                        w.wait("tensor", s3_["ee"], w.val(s3_["ee"]) - 1)
                        if w.on("tensor"):
                            eng.matmul(rbank[:, 0:512],
                                       cli[0:2, j * 128:(j + 1) * 128],
                                       cli2[0:2, gi * 512:(gi + 1) * 512],
                                       start=True, stop=True
                                       ).then_inc(s3_["rmm"], 1)
                        w.bump(s3_["rmm"], 1)
                        w.wait("scalar", s3_["rmm"])
                        w.wait("scalar", s3_["wb"], w.val(s3_["wb"]) - 1)
                        if w.on("scalar"):
                            eng.activation(out=rbank[:, 0:512],
                                           in_=rbank[:, 0:512], func=AF.Relu)
                            eng.drain()
                            eng.activation(out=ebuf[:], in_=rbank[:, 0:512],
                                           func=AF.Exp, scale=-1.0
                                           ).then_inc(s3_["ee"], 1)
                        w.bump(s3_["ee"], 1)
                        wblk = Wblk2[j % 2]
                        w.wait("vector", s3_["smm"])
                        w.wait("vector", s3_["ee"])
                        w.wait("vector", s3_["pvm"], w.val(s3_["pvm"]) - 1)
                        ins = None
                        if w.on("vector"):
                            eng.drain()
                            ins = eng.scalar_tensor_tensor(
                                out=wblk[:], in0=sbank[:, 0:512], scalar=0.0,
                                in1=ebuf[:], op0=ALU.bypass, op1=ALU.mult)
                        if j < 4 * gi + 4:
                            jq = j - 4 * gi
                            if w.on("vector"):
                                eng.drain()
                                ins = eng.tensor_tensor(
                                    out=wblk[:, jq * 128:(jq + 1) * 128],
                                    in0=wblk[:, jq * 128:(jq + 1) * 128],
                                    in1=MASKLT[:], op=ALU.mult)
                                for q2 in range(jq + 1, 4):
                                    ins = eng.memset(
                                        wblk[:, q2 * 128:(q2 + 1) * 128], 0.0)
                        if w.on("vector"):
                            eng.drain()
                            ins.then_inc(s3_["wb"], 1)
                        w.bump(s3_["wb"], 1)
                        w.wait("tensor", s3_["wb"])
                        if w.on("tensor"):
                            eng.matmul(PSF[3][bh:bh + 64, 0:512],
                                       vtm[:, j * 128 + bh:j * 128 + bh + 64],
                                       wblk[:],
                                       start=False, stop=False
                                       ).then_inc(s3_["pvm"], 1)
                        w.bump(s3_["pvm"], 1)
                    for q in range(4):
                        i = 4 * gi + q
                        w.wait("vector", s3_["pvm"])
                        if w.on("vector"):
                            eng.drain()
                            eng.tensor_scalar(out=term[:, bh:bh + 64],
                                              in0=vtm[:, i * 128 + bh:
                                                      i * 128 + bh + 64],
                                              scalar1=betaT[:, i * 16 + h:
                                                            i * 16 + h + 1],
                                              scalar2=None, op0=ALU.mult
                                              ).then_inc(s3_["wb"], 1)
                        w.bump(s3_["wb"], 1)
                        w.wait("tensor", s3_["wb"])
                        if w.on("tensor"):
                            eng.matmul(PSF[3][:, q * 128:(q + 1) * 128],
                                       term[:], IDENTF[:], is_transpose=True,
                                       start=False, stop=(q == 3)
                                       ).then_inc(s3_["pvm"], 1)
                        w.bump(s3_["pvm"], 1)
                    w.wait("vector", s3_["pvm"])
                    if w.on("vector"):
                        eng.tensor_copy(
                            out=AT[bh:bh + 64,
                                   ec * T + gi * 512:ec * T + (gi + 1) * 512],
                            in_=PSF[3][bh:bh + 64, 0:512]).then_inc(s3_["pve"], 1)
                    w.bump(s3_["pve"], 1)

            load_w(ext["wout1"])
            w.wait("tensor", s3_["wfree"])
            w.wait("tensor", s3_["pve"])
            for tch in range(4):
                if tch >= 1:
                    w.wait("vector", s3_["tr2"], w.val(s3_["tr2"]))
                for co in range(CCN):
                    mmi = w.val(s3_["pj"])
                    pband = PSF[mmi % 2]
                    if mmi >= 2:
                        w.wait("tensor", s3_["pje"], mmi - 1)
                    if w.on("tensor"):
                        for ck in range(CCN):
                            ins = eng.matmul(
                                pband[:, 0:512],
                                wpb[:, ck * C + co * 128:ck * C + (co + 1) * 128],
                                AT[:, ck * T + tch * 512:ck * T + (tch + 1) * 512],
                                start=(ck == 0), stop=(ck == CCN - 1))
                        ins.then_inc(s3_["pj"], 1)
                    w.bump(s3_["pj"], 1)
                    w.wait("vector", s3_["pj"])
                    if w.on("vector"):
                        eng.tensor_scalar(out=yfb[:, co * 512:(co + 1) * 512],
                                          in0=pband[:, 0:512],
                                          scalar1=mc(1, 2, co, b),
                                          scalar2=None, op0=ALU.mult
                                          ).then_inc(s3_["pje"], 1)
                    w.bump(s3_["pje"], 1)
                for st_ in range(4):
                    tt = tch * 4 + st_
                    gx = w.val(s3_["xnew"])
                    w.wait("tensor", s3_["pje"])
                    if gx >= 1:
                        w.wait("tensor", s3_["xnew"], gx)
                    if w.on("tensor"):
                        for co in range(CCN):
                            ins = eng.matmul(PSB[1][:, co * 128:(co + 1) * 128],
                                             yfb[:, co * 512 + st_ * 128:
                                                 co * 512 + (st_ + 1) * 128],
                                             IDENTB[:], is_transpose=True,
                                             start=(co == 0), stop=(co == CCN - 1))
                        ins.then_inc(s3_["tr2"], 1)
                    w.bump(s3_["tr2"], 1)
                    if gx >= 1:
                        w.wait("sync", s3_["xnew"], gx)
                        w.wait("sync", s3_["xst"], w.val(s3_["xst"]))
                    if tch == 0 and st_ == 0:
                        w.wait("sync", s3_["wfree"],
                               (w.val(s3_["wld"]) + w.val(s3_["wld2"])) // 16)
                    if w.on("sync"):
                        eng.dma_start(out=xa3[:],
                                      in_=sc["X"][b, tt * 128:(tt + 1) * 128, :]
                                      ).then_inc(s3_["xld"], 16)
                    w.bump(s3_["xld"], 16)
                    w.wait("vector", s3_["xld"])
                    w.wait("vector", s3_["tr2"])
                    if w.on("vector"):
                        eng.tensor_tensor(out=xa3[:], in0=xa3[:],
                                          in1=PSB[1][:, 0:1024], op=ALU.add
                                          ).then_inc(s3_["xnew"], 1)
                    w.bump(s3_["xnew"], 1)
                    w.wait("gpsimd", s3_["xnew"])
                    if w.on("gpsimd"):
                        eng.dma_start(out=sc["X"][b, tt * 128:(tt + 1) * 128, :],
                                      in_=xa3[:]).then_inc(s3_["xst"], 16)
                        if "X3" in tap_aps:
                            eng.dma_start(
                                out=tap_aps["X3"][b, tt * 128:(tt + 1) * 128, :],
                                in_=xa3[:]).then_inc(s3_["xst"], 16)
                    w.bump(s3_["xst"], 16 * ntap)
            w.wait("sync", s3_["xst"])
            w.wait("vector", s3_["xst"])
            w.wait("gpsimd", s3_["xst"])
        return w

    add_phase("P3", walk)


def _l1_ffn(S):
    nc, sem, ext, sc, tap_aps = S["nc"], S["sem"], S["ext"], S["sc"], S["tap_aps"]
    out = S["out"]
    CP, mc = S["CP"], S["mc"]
    IDENTB, IDENTF, ONESROW = S["IDENTB"], S["IDENTF"], S["ONESROW"]
    AR2, AR4, PSF, PSB = S["AR2"], S["AR4"], S["PSF"], S["PSB"]
    add_phase = S["add_phase"]

    JC = I1 // 128
    OCW = [512] * 5 + [256]

    o2 = [0]

    def a2(cols):
        s = o2[0]
        o2[0] += cols
        assert o2[0] <= AR2_COLS, f"AR2 overflow {o2[0]}"
        return AR2[:, s:s + cols]

    o4 = [0]

    def a4(cols):
        s = o4[0]
        o4[0] += cols
        assert o4[0] <= AR4_COLS, f"AR4 overflow {o4[0]}"
        return AR4[:, s:s + cols]

    wgq = a2(CCN * I1)
    wuq = a2(CCN * I1)
    wdq = a2(JC * C)
    xqT = [a2(CCN * 128), a2(CCN * 128)]
    h2q = [a2(I1), a2(I1)]
    h2qT = a2(I1)

    xa4 = a4(C)
    bcA = a4(C)
    bcB = a4(C)
    bcG = a4(C)
    wn2 = a4(C)
    wn = [wn2[:, 0:512], wn2[:, 512:1024]]
    wcb = [a4(512), a4(512)]
    anb = [a4(512), a4(512)]
    eb_ = a4(512)
    sgm = a4(512)
    AR4X = S["AR4X"]
    gb_ = AR4X[:, 0:512]
    qcols = [AR4X[:, 512:576], AR4X[:, 576:640]]
    lnc = AR4X[:, 640:648]

    _s4map = {"xld": "xld", "wldA": "wld", "wldB": "wld2", "ost": "xld2",
              "s1": "qk", "s2": "qke", "s3": "sgrp", "s4": "sexp",
              "s5": "smsk", "s6": "pvm", "s7": "pvv", "qv": "od",
              "varc": "h0", "inv": "lninv",
              "tra": "ao", "eva": "aoT", "trh": "atc", "evh": "tr2",
              "gmm": "pj", "sg": "pje", "umm": "xnew", "sil": "mm",
              "h2s": "vt", "dmm": "bv", "res": "dq",
              "bct": "dqt", "bcr": "dqe", "bcb": "q2", "bce": "mx"}
    s4 = {k: sem(v) for k, v in _s4map.items()}

    def walk(w):
        eng = w.eng
        Bs = {k: w.val(s) for k, s in s4.items()}
        st = {}
        qn = [0]
        pend = []

        w.wait("sync", sem("xst"), w.val(sem("xst")))

        def head(vap, cw, dst, dstg, vap_fresh, vwaits=()):
            c = qn[0]
            p = c % 2
            qc = qcols[p]
            nb = cw // 16
            for sm, vv in vwaits:
                w.wait("vector", sm, vv)
            if c >= 2:
                w.wait("vector", s4["s5"], Bs["s5"] + c - 1)
            if w.on("vector"):
                eng.drain()
                eng.tensor_reduce(out=qc[:, 0:nb],
                                  in_=vap.rearrange("p (nb k) -> p nb k", k=16),
                                  axis=AX.X, op=ALU.max,
                                  apply_absolute_value=True)
                eng.drain()
                eng.tensor_scalar(out=qc[:, 0:nb], in0=qc[:, 0:nb],
                                  scalar1=1.0 / 6.0, scalar2=1e-12,
                                  op0=ALU.mult, op1=ALU.max
                                  ).then_inc(s4["s1"], 1)
            w.bump(s4["s1"], 1)
            w.wait("scalar", s4["s1"], Bs["s1"] + c + 1)
            if c >= 2:
                w.wait("scalar", s4["qv"], Bs["qv"] + c - 1)
            if w.on("scalar"):
                eng.activation(out=qc[:, 32:32 + nb], in_=qc[:, 0:nb],
                               func=AF.Ln)
                eng.drain()
                eng.activation(out=qc[:, 32:32 + nb], in_=qc[:, 32:32 + nb],
                               func=AF.Exp, scale=-1.0).then_inc(s4["s2"], 1)
            w.bump(s4["s2"], 1)
            w.wait("vector", s4["s2"], Bs["s2"] + c + 1)
            if w.on("vector"):
                eng.tensor_tensor(
                    out=wn[p][:, 0:cw].rearrange("p (nb k) -> p nb k", k=16),
                    in0=vap.rearrange("p (nb k) -> p nb k", k=16),
                    in1=qc[:, 32:32 + nb].unsqueeze(-1).broadcast_to(
                        [128, nb, 16]),
                    op=ALU.mult)
                eng.drain()
                eng.tensor_scalar(out=wcb[p][:, 0:cw], in0=wn[p][:, 0:cw],
                                  scalar1=2.25, scalar2=-2.25,
                                  op0=ALU.min, op1=ALU.max
                                  ).then_inc(s4["s4"], 1)
                eng.scalar_tensor_tensor(out=anb[p][:, 0:cw],
                                         in0=wn[p][:, 0:cw], scalar=-1.0,
                                         in1=wn[p][:, 0:cw],
                                         op0=ALU.mult, op1=ALU.max
                                         ).then_inc(s4["s3"], 1)
            w.bump(s4["s4"], 1)
            w.bump(s4["s3"], 1)
            w.wait("scalar", s4["s4"], Bs["s4"] + c + 1)
            if w.on("scalar"):
                eng.activation(out=wcb[p][:, 0:cw], in_=wcb[p][:, 0:cw],
                               func=AF.Identity, scale=2.0,
                               bias=CP[:, CP_MAG:CP_MAG + 1])
                eng.drain()
                eng.activation(out=wcb[p][:, 0:cw], in_=wcb[p][:, 0:cw],
                               func=AF.Identity, scale=1.0,
                               bias=CP[:, CP_NMAG:CP_NMAG + 1]
                               ).then_inc(s4["s5"], 1)
            w.bump(s4["s5"], 1)
            pend.append((c, cw, dst, dstg))
            qn[0] += 1

        def tail():
            c, cw, dst, dstg = pend.pop(0)
            p = c % 2
            qc = qcols[p]
            nb = cw // 16
            w.wait("vector", s4["s5"], Bs["s5"] + c + 1)
            for sm, vv in dstg:
                w.wait("vector", sm, vv)
            if w.on("vector"):
                eng.tensor_scalar(out=gb_[:, 0:cw], in0=anb[p][:, 0:cw],
                                  scalar1=2.5, scalar2=None, op0=ALU.is_gt)
                eng.tensor_scalar(out=eb_[:, 0:cw], in0=anb[p][:, 0:cw],
                                  scalar1=3.5, scalar2=None, op0=ALU.is_gt)
                eng.tensor_scalar(out=wn[p][:, 0:cw], in0=wn[p][:, 0:cw],
                                  scalar1=0.0, scalar2=2.0,
                                  op0=ALU.is_ge, op1=ALU.mult)
                eng.drain()
                eng.tensor_tensor(out=gb_[:, 0:cw], in0=gb_[:, 0:cw],
                                  in1=eb_[:, 0:cw], op=ALU.add)
                eng.tensor_scalar(out=anb[p][:, 0:cw], in0=anb[p][:, 0:cw],
                                  scalar1=5.0, scalar2=2.0,
                                  op0=ALU.is_gt, op1=ALU.mult)
                eng.drain()
                eng.tensor_tensor(out=gb_[:, 0:cw], in0=gb_[:, 0:cw],
                                  in1=anb[p][:, 0:cw], op=ALU.add)
                eng.drain()
                eng.scalar_tensor_tensor(out=eb_[:, 0:cw],
                                         in0=wn[p][:, 0:cw], scalar=-1.0,
                                         in1=gb_[:, 0:cw],
                                         op0=ALU.add, op1=ALU.mult)
                eng.drain()
                eng.scalar_tensor_tensor(out=wcb[p][:, 0:cw],
                                         in0=wcb[p][:, 0:cw], scalar=0.5,
                                         in1=eb_[:, 0:cw],
                                         op0=ALU.mult, op1=ALU.add)
                eng.drain()
                eng.tensor_tensor(
                    out=dst.rearrange("p (nb k) -> p nb k", k=16),
                    in0=wcb[p][:, 0:cw].rearrange("p (nb k) -> p nb k", k=16),
                    in1=qc[:, 0:nb].unsqueeze(-1).broadcast_to([128, nb, 16]),
                    op=ALU.mult).then_inc(s4["qv"], 1)
            w.bump(s4["qv"], 1)

        wslots = [xa4, bcA]
        wsems = [s4["wldA"], s4["wldB"]]
        wln = [0]

        def wload(src, width):
            i = wln[0]
            slot = i % 2
            lsm = wsems[slot]
            if i >= 2:
                lastc = st[("wplast", i - 2)]
                w.wait("sync", s4["qv"], Bs["qv"] + lastc + 1)
            if w.on("sync"):
                eng.dma_start(out=wslots[slot][:, 0:width], in_=src
                              ).then_inc(lsm, 16)
            w.bump(lsm, 16)
            wln[0] += 1
            return slot, lsm

        wot = [0]
        for wname, notl, K, dst2, pitch in [("f1_gate", JC, C, wgq, I1),
                                            ("f1_up", JC, C, wuq, I1),
                                            ("f1_down", CCN, I1, wdq, C)]:
            for ot in range(notl):
                wslot = wot[0]
                c0 = qn[0]
                for pi in range((K + 1023) // 1024):
                    pw = min(1024, K - pi * 1024)
                    slot, lsm = wload(
                        ext[wname][ot * 128:(ot + 1) * 128,
                                   pi * 1024:pi * 1024 + pw], pw)
                    lv = w.val(lsm)
                    co = 0
                    while co < pw:
                        cw = min(512, pw - co)
                        head(vap=wslots[slot][:, co:co + cw], cw=cw,
                             dst=h2q[wslot % 2][:, pi * 1024 + co:
                                                pi * 1024 + co + cw],
                             dstg=[(s4["tra"], st.get(("traW", wslot - 2), 0))],
                             vap_fresh=False, vwaits=[(lsm, lv)])
                        if len(pend) > 1:
                            tail()
                        co += cw
                    st[("wplast", wln[0] - 1)] = qn[0] - 1
                while pend:
                    tail()
                nblk = K // 128
                for gr0 in range(0, nblk, 8):
                    cnt = min(8, nblk - gr0)
                    lastc = c0 + ((gr0 + cnt) * 128 + 511) // 512 - 1
                    w.wait("tensor", s4["qv"], Bs["qv"] + lastc + 1)
                    w.wait("tensor", s4["eva"], w.val(s4["eva"]))
                    if w.on("tensor"):
                        for j in range(cnt):
                            ins = eng.matmul(
                                PSB[0][:, j * 128:(j + 1) * 128],
                                h2q[wslot % 2][:, (gr0 + j) * 128:
                                               (gr0 + j + 1) * 128],
                                IDENTB[:], is_transpose=True,
                                start=(j == 0), stop=(j == cnt - 1))
                        ins.then_inc(s4["tra"], 1)
                    w.bump(s4["tra"], 1)
                    w.wait("scalar", s4["tra"])
                    if w.on("scalar"):
                        for j in range(cnt):
                            ins = eng.activation(
                                out=dst2[:, (gr0 + j) * pitch + ot * 128:
                                         (gr0 + j) * pitch + ot * 128 + 128],
                                in_=PSB[0][:, j * 128:(j + 1) * 128],
                                func=AF.Identity, scale=1.0)
                        ins.then_inc(s4["eva"], 1)
                    w.bump(s4["eva"], 1)
                st[("traW", wslot)] = w.val(s4["tra"])
                wot[0] += 1
        st["Wdone"] = w.val(s4["eva"])
        st[("tra", -2)] = st[("tra", -1)] = w.val(s4["tra"])
        wchunks = qn[0]

        for b in range(B):
            for ci3, (jm, dst3) in enumerate([(4, bcA), (3, bcB), (5, bcG)]):
                if ci3 > 0 or b > 0:
                    w.wait("tensor", s4["bcr"], w.val(s4["bcr"]))
                    w.wait("tensor", s4["bce"], w.val(s4["bce"]))
                if b > 0 and ci3 == 0:
                    gp = b * TTN - 1
                    w.wait("tensor", s4["sg"], st[("sgg", gp)])
                    w.wait("tensor", s4["sil"], st[("silg", gp)])
                    w.wait("tensor", s4["h2s"], st[("h2g", gp)])
                if w.on("tensor"):
                    for ck in range(CCN):
                        bank = PSF[0] if ck < 4 else PSF[1]
                        ins = eng.matmul(
                            bank[0:1, (ck % 4) * 128:(ck % 4 + 1) * 128],
                            mc(1, jm, ck, b), IDENTF[:], is_transpose=True,
                            start=(ck % 4 == 0), stop=(ck % 4 == 3))
                    ins.then_inc(s4["bct"], 1)
                w.bump(s4["bct"], 1)
                w.wait("scalar", s4["bct"])
                if ci3 == 0:
                    w.wait("scalar", s4["qv"], Bs["qv"] + qn[0])
                    w.wait("scalar", s4["ost"], w.val(s4["ost"]))
                    w.wait("scalar", s4["res"], w.val(s4["res"]))
                if w.on("scalar"):
                    eng.activation(out=xa4[0:1, 0:512], in_=PSF[0][0:1, 0:512],
                                   func=AF.Identity, scale=1.0)
                    ins = eng.activation(out=xa4[0:1, 512:1024],
                                         in_=PSF[1][0:1, 0:512],
                                         func=AF.Identity, scale=1.0)
                    ins.then_inc(s4["bcr"], 1)
                w.bump(s4["bcr"], 1)
                w.wait("tensor", s4["bcr"])
                if w.on("tensor"):
                    eng.matmul(PSF[2][:, 0:512], ONESROW[0:1, 0:128],
                               xa4[0:1, 0:512], start=True, stop=True)
                    ins = eng.matmul(PSF[3][:, 0:512], ONESROW[0:1, 0:128],
                                     xa4[0:1, 512:1024], start=True, stop=True)
                    ins.then_inc(s4["bcb"], 1)
                w.bump(s4["bcb"], 1)
                w.wait("vector", s4["bcb"])
                if b > 0 and ci3 == 0:
                    pass
                if w.on("vector"):
                    eng.tensor_copy(out=dst3[:, 0:512], in_=PSF[2][:, 0:512])
                    ins = eng.tensor_copy(out=dst3[:, 512:1024],
                                          in_=PSF[3][:, 0:512])
                    ins.then_inc(s4["bce"], 1)
                w.bump(s4["bce"], 1)
            st[("bce", b)] = w.val(s4["bce"])

            for tt in range(TTN):
                g = b * TTN + tt
                if g == 0:
                    w.wait("sync", s4["qv"], Bs["qv"] + wchunks)
                else:
                    w.wait("sync", s4["ost"], st[("ost", g - 1)])
                w.wait("sync", s4["bcb"], w.val(s4["bcb"]))
                w.wait("sync", s4["bcr"], w.val(s4["bcr"]))
                if w.on("sync"):
                    eng.dma_start(out=xa4[:],
                                  in_=sc["X"][b, tt * 128:(tt + 1) * 128, :]
                                  ).then_inc(s4["xld"], 16)
                w.bump(s4["xld"], 16)
                w.wait("vector", s4["xld"])
                if w.on("vector"):
                    eng.tensor_reduce(out=lnc[:, 0:1], in_=xa4[:], axis=AX.X,
                                      op=ALU.add)
                    eng.scalar_tensor_tensor(out=wn2[:], in0=xa4[:],
                                             scalar=1.0, in1=xa4[:],
                                             op0=ALU.mult, op1=ALU.mult,
                                             accum_out=lnc[:, 1:2])
                    eng.drain()
                    eng.tensor_scalar(out=lnc[:, 2:3], in0=lnc[:, 0:1],
                                      scalar1=1.0 / C, scalar2=None,
                                      op0=ALU.mult)
                    eng.tensor_scalar(out=lnc[:, 4:5], in0=lnc[:, 1:2],
                                      scalar1=1.0 / C, scalar2=LN_EPS,
                                      op0=ALU.mult, op1=ALU.add)
                    eng.drain()
                    eng.scalar_tensor_tensor(out=lnc[:, 3:4], in0=lnc[:, 2:3],
                                             scalar=-1.0, in1=lnc[:, 2:3],
                                             op0=ALU.mult, op1=ALU.mult)
                    eng.drain()
                    eng.tensor_tensor(out=lnc[:, 4:5], in0=lnc[:, 4:5],
                                      in1=lnc[:, 3:4], op=ALU.add
                                      ).then_inc(s4["varc"], 1)
                w.bump(s4["varc"], 1)
                w.wait("scalar", s4["varc"])
                if w.on("scalar"):
                    eng.activation(out=lnc[:, 5:6], in_=lnc[:, 4:5], func=AF.Ln)
                    eng.drain()
                    eng.activation(out=lnc[:, 6:7], in_=lnc[:, 5:6],
                                   func=AF.Exp, scale=-0.5
                                   ).then_inc(s4["inv"], 1)
                w.bump(s4["inv"], 1)
                w.wait("vector", s4["inv"])
                w.wait("vector", s4["bce"], st[("bce", b)])
                if w.on("vector"):
                    eng.scalar_tensor_tensor(out=lnc[:, 7:8], in0=lnc[:, 2:3],
                                             scalar=-1.0, in1=lnc[:, 6:7],
                                             op0=ALU.mult, op1=ALU.mult)
                    eng.drain()
                dga = [(s4["tra"], st.get(("tra", g - 2), 0)),
                       (s4["trh"], st.get(("trh", g - 2), 0))]
                for ci in range(2):
                    p = qn[0] % 2
                    sl = slice(ci * 512, (ci + 1) * 512)
                    if w.on("vector"):
                        eng.tensor_scalar(out=wn[p][:], in0=xa4[:, sl],
                                          scalar1=lnc[:, 6:7],
                                          scalar2=lnc[:, 7:8],
                                          op0=ALU.mult, op1=ALU.add)
                        eng.drain()
                        eng.tensor_tensor(out=wn[p][:], in0=wn[p][:],
                                          in1=bcA[:, sl], op=ALU.mult)
                        eng.drain()
                        eng.tensor_tensor(out=wn[p][:], in0=wn[p][:],
                                          in1=bcB[:, sl], op=ALU.add)
                    head(vap=wn[p][:], cw=512,
                         dst=h2q[g % 2][:, ci * 512:(ci + 1) * 512],
                         dstg=dga, vap_fresh=True)
                    if len(pend) > 1:
                        tail()
                while pend:
                    tail()
                w.wait("tensor", s4["qv"], Bs["qv"] + qn[0])
                w.wait("tensor", s4["eva"],
                       st[("eva", g - 1)] if g >= 1 else st["Wdone"])
                if w.on("tensor"):
                    for ck in range(CCN):
                        ins = eng.matmul(PSB[0][:, ck * 128:(ck + 1) * 128],
                                         h2q[g % 2][:, ck * 128:(ck + 1) * 128],
                                         IDENTB[:], is_transpose=True,
                                         start=(ck == 0), stop=(ck == CCN - 1))
                    ins.then_inc(s4["tra"], 1)
                w.bump(s4["tra"], 1)
                st[("tra", g)] = w.val(s4["tra"])
                w.wait("scalar", s4["tra"])
                if g >= 2:
                    w.wait("scalar", s4["umm"], st[("umm", g - 2)])
                if w.on("scalar"):
                    eng.activation(out=xqT[g % 2][:], in_=PSB[0][:, 0:1024],
                                   func=AF.Identity, scale=1.0
                                   ).then_inc(s4["eva"], 1)
                w.bump(s4["eva"], 1)
                st[("eva", g)] = w.val(s4["eva"])

                ocoff = 0
                c0 = qn[0]
                dgc = [(s4["tra"], st[("tra", g)]),
                       (s4["trh"], st.get(("trh", g - 2), 0))]
                for oc in range(6):
                    ow = OCW[oc]
                    gbk = PSF[oc % 2]
                    ubk = PSF[2 + oc % 2]
                    w.wait("tensor", s4["eva"], st[("eva", g)])
                    if oc >= 2:
                        w.wait("tensor", s4["sil"], st[("silb", g, oc - 2)])
                        w.wait("tensor", s4["sg"], st[("sgb", g, oc - 2)])
                    elif g >= 1:
                        w.wait("tensor", s4["sil"], st[("silg", g - 1)])
                        w.wait("tensor", s4["sg"], st[("sgg", g - 1)])
                    if tt == 0 and oc < 2:
                        w.wait("tensor", s4["bcr"], w.val(s4["bcr"]))
                    if w.on("tensor"):
                        for ck in range(CCN):
                            ins = eng.matmul(
                                gbk[:, 0:ow],
                                xqT[g % 2][:, ck * 128:(ck + 1) * 128],
                                wgq[:, ck * I1 + ocoff:ck * I1 + ocoff + ow],
                                start=(ck == 0), stop=(ck == CCN - 1))
                        ins.then_inc(s4["gmm"], 1)
                    w.bump(s4["gmm"], 1)
                    if oc >= 2:
                        w.wait("tensor", s4["h2s"], st[("h2v", g, oc - 2)])
                    elif g >= 1:
                        w.wait("tensor", s4["h2s"], st[("h2g", g - 1)])
                    if tt == 0 and oc < 2:
                        w.wait("tensor", s4["bce"], st[("bce", b)])
                    if w.on("tensor"):
                        for ck in range(CCN):
                            ins = eng.matmul(
                                ubk[:, 0:ow],
                                xqT[g % 2][:, ck * 128:(ck + 1) * 128],
                                wuq[:, ck * I1 + ocoff:ck * I1 + ocoff + ow],
                                start=(ck == 0), stop=(ck == CCN - 1))
                        ins.then_inc(s4["umm"], 1)
                    w.bump(s4["umm"], 1)
                    w.wait("scalar", s4["gmm"])
                    w.wait("scalar", s4["sil"], w.val(s4["sil"]))
                    if w.on("scalar"):
                        eng.activation(out=sgm[:, 0:ow], in_=gbk[:, 0:ow],
                                       func=AF.Sigmoid).then_inc(s4["sg"], 1)
                    w.bump(s4["sg"], 1)
                    for sl in range(max(1, ow // 512)):
                        cw = min(512, ow)
                        p = qn[0] % 2
                        psl = slice(sl * 512, sl * 512 + cw)
                        w.wait("vector", s4["sg"])
                        w.wait("vector", s4["umm"])
                        if w.on("vector"):
                            eng.tensor_tensor(out=wn[p][:, 0:cw],
                                              in0=gbk[:, psl],
                                              in1=sgm[:, psl],
                                              op=ALU.mult).then_inc(s4["sil"], 1)
                            eng.drain()
                            eng.tensor_tensor(out=wn[p][:, 0:cw],
                                              in0=wn[p][:, 0:cw],
                                              in1=ubk[:, psl],
                                              op=ALU.mult).then_inc(s4["h2s"], 1)
                        w.bump(s4["sil"], 1)
                        w.bump(s4["h2s"], 1)
                        head(vap=wn[p][:, 0:cw], cw=cw,
                             dst=h2q[g % 2][:, ocoff + sl * 512:
                                            ocoff + sl * 512 + cw],
                             dstg=dgc, vap_fresh=True)
                        if len(pend) > 1:
                            tail()
                    st[("silb", g, oc)] = w.val(s4["sil"])
                    st[("sgb", g, oc)] = w.val(s4["sg"])
                    st[("h2v", g, oc)] = w.val(s4["h2s"])
                    ocoff += ow
                st[("silg", g)] = w.val(s4["sil"])
                st[("sgg", g)] = w.val(s4["sg"])
                st[("h2g", g)] = w.val(s4["h2s"])
                st[("umm", g)] = w.val(s4["umm"])
                while pend:
                    tail()
                for gr in range(3):
                    gr0 = gr * 8
                    cnt = min(8, JC - gr0)
                    lastc = c0 + ((gr0 + cnt) * 128 + 511) // 512 - 1
                    w.wait("tensor", s4["qv"], Bs["qv"] + lastc + 1)
                    w.wait("tensor", s4["evh"], w.val(s4["evh"]))
                    if w.on("tensor"):
                        for j in range(cnt):
                            ins = eng.matmul(
                                PSB[1][:, j * 128:(j + 1) * 128],
                                h2q[g % 2][:, (gr0 + j) * 128:
                                            (gr0 + j + 1) * 128],
                                IDENTB[:], is_transpose=True,
                                start=(j == 0), stop=(j == cnt - 1))
                        ins.then_inc(s4["trh"], 1)
                    w.bump(s4["trh"], 1)
                    w.wait("scalar", s4["trh"])
                    if gr == 0 and g >= 1:
                        w.wait("scalar", s4["dmm"], st[("dmm", g - 1)])
                    if w.on("scalar"):
                        ins = eng.activation(
                            out=h2qT[:, gr0 * 128:(gr0 + cnt) * 128],
                            in_=PSB[1][:, 0:cnt * 128],
                            func=AF.Identity, scale=1.0)
                        ins.then_inc(s4["evh"], 1)
                    w.bump(s4["evh"], 1)
                st[("trh", g)] = w.val(s4["trh"])
                st[("evh", g)] = w.val(s4["evh"])

                w.wait("tensor", s4["evh"], st[("evh", g)])
                if g >= 1:
                    w.wait("tensor", s4["res"], st[("res", g - 1)])
                if w.on("tensor"):
                    for cc in range(2):
                        bank = PSF[4 + cc]
                        for jc in range(JC):
                            ins = eng.matmul(
                                bank[:, 0:512],
                                h2qT[:, jc * 128:(jc + 1) * 128],
                                wdq[:, jc * C + cc * 512:
                                    jc * C + cc * 512 + 512],
                                start=(jc == 0), stop=(jc == JC - 1))
                        ins.then_inc(s4["dmm"], 1)
                w.bump(s4["dmm"], 2)
                st[("dmm", g)] = w.val(s4["dmm"])
                w.wait("vector", s4["dmm"])
                if w.on("vector"):
                    eng.tensor_tensor(out=wn[0][:], in0=PSF[4][:, 0:512],
                                      in1=bcG[:, 0:512], op=ALU.mult)
                    eng.tensor_tensor(out=wn[1][:], in0=PSF[5][:, 0:512],
                                      in1=bcG[:, 512:1024], op=ALU.mult)
                    eng.drain()
                    eng.tensor_tensor(out=xa4[:, 0:512], in0=xa4[:, 0:512],
                                      in1=wn[0][:], op=ALU.add)
                    ins = eng.tensor_tensor(out=xa4[:, 512:1024],
                                            in0=xa4[:, 512:1024],
                                            in1=wn[1][:], op=ALU.add)
                    ins.then_inc(s4["res"], 1)
                w.bump(s4["res"], 1)
                st[("res", g)] = w.val(s4["res"])
                w.wait("sync", s4["res"])
                if w.on("sync"):
                    eng.dma_start(out=out[b, tt * 128:(tt + 1) * 128, :],
                                  in_=xa4[:]).then_inc(s4["ost"], 16)
                w.bump(s4["ost"], 16)
                st[("ost", g)] = w.val(s4["ost"])

        w.wait("sync", s4["ost"])
        w.wait("vector", s4["ost"])
        return w

    add_phase("P4", walk)



def _host_consts():
    con = {}
    con["consts"] = np.asarray([CPV], dtype=np.float32)
    con["onesrow"] = np.ones((1, 128), dtype=np.float32)
    con["identb"] = np.eye(128, dtype=ml_dtypes.bfloat16)
    con["identf"] = np.eye(128, dtype=np.float32)
    con["iota_col"] = np.arange(128, dtype=np.float32).reshape(128, 1)
    ds = np.arange(128, dtype=np.float32).reshape(128, 1)
    dl = np.arange(16, dtype=np.float32).reshape(1, 16)
    con["iota_delta"] = (ds - 128.0 * dl).astype(np.float32)
    ii = np.arange(128).reshape(128, 1)
    jj = np.arange(128).reshape(1, 128)
    con["mask_ut"] = (ii <= jj).astype(ml_dtypes.bfloat16)
    con["mask_lt"] = (ii >= jj).astype(ml_dtypes.bfloat16)
    sel2 = np.zeros((128, 2), np.float32)
    sel2[0:64, 0] = 1.0
    sel2[64:128, 1] = 1.0
    con["sel2"] = sel2.astype(ml_dtypes.bfloat16)
    con["sel2b"] = np.ascontiguousarray(sel2.T)
    selcli = np.zeros((17, 32), np.float32)
    selcli2 = np.zeros((17, 32), np.float32)
    for h in range(16):
        selcli[h, 2 * h] = 1.0
        selcli[16, 2 * h + 1] = 1.0
        selcli2[16, 2 * h] = 1.0
        selcli2[h, 2 * h + 1] = -1.0
    con["selcli"] = selcli
    con["selcli2"] = selcli2
    return con


def _shard_inputs(inputs):
    con = _host_consts()
    b_attn = np.asarray(inputs["L0_attn_b_attn"], dtype=np.float32)
    bqk_q = b_attn[0:C].reshape(H // 2, 128).T
    bqk_k = b_attn[C:2 * C].reshape(H // 2, 128).T
    bv_pairs = b_attn[2 * C:3 * C].reshape(1, 8 * 128)
    bproj_cols = np.asarray(inputs["L0_attn_b_proj"],
                            dtype=np.float32).reshape(CCN, 128).T
    maps = []
    for ci in range(NCORES):
        rows = slice(2 * ci, 2 * ci + 2)
        m = {
            "x": inputs["L0_attn_x"][rows],
            "w_attn": inputs["L0_attn_w_attn"],
            "bqk_q": bqk_q, "bqk_k": bqk_k, "bv_pairs": bv_pairs,
            "w_proj": inputs["L0_attn_w_proj"], "bproj_cols": bproj_cols,
            "f0_gate": inputs["L0_ffn_w_gate"], "f0_up": inputs["L0_ffn_w_up"],
            "f0_down": inputs["L0_ffn_w_down"], "c0": inputs["L0_c"][rows],
            "wmod0": inputs["L0_wmod"], "wk1": inputs["L1_attn_Wk"],
            "wv1": inputs["L1_attn_Wv"], "wq1": inputs["L1_attn_Wq"],
            "wbeta1": inputs["L1_attn_Wbeta"], "wout1": inputs["L1_attn_Wout"],
            "f1_gate": inputs["L1_ffn_w_gate"], "f1_up": inputs["L1_ffn_w_up"],
            "f1_down": inputs["L1_ffn_w_down"], "c1": inputs["L1_c"][rows],
            "wmod1": inputs["L1_wmod"],
        }
        m.update(con)
        maps.append({k: np.ascontiguousarray(np.asarray(v)) for k, v in m.items()})
    return maps


_NC_CACHE = {}


def _get_program(n_macro=99, taps=()):
    key = (n_macro, tuple(sorted(taps)))
    if key not in _NC_CACHE:
        _NC_CACHE[key] = build_program(n_macro, taps)
    return _NC_CACHE[key]


def kernel(**inputs):
    nc = _get_program()
    maps = _shard_inputs(inputs)
    res = run_bass_kernel_spmd(nc, maps, core_ids=list(range(NCORES)))
    outs = [np.asarray(r["out"]) for r in res.results]
    return np.concatenate(outs, axis=0)
